# version 02
# baseline (speedup 1.0000x reference)
import contextlib
import numpy as np
import ml_dtypes
import concourse.bass as bass
import concourse.mybir as mybir
import concourse.expressions as ex
from bass_rust import RuntimeValue
from concourse.bass_utils import run_bass_kernel_spmd

F32 = mybir.dt.float32
BF16 = mybir.dt.bfloat16
I32 = mybir.dt.int32
AF = mybir.ActivationFunctionType
ALU = mybir.AluOpType

D = 1024
SEQ = 4096
NB = 4
CH = 512
NCH = 8
TLOC = 2048
NHP = 8
DH = 64
HID = 3072
EPS = 1e-5
ROLE_CHUNKS = [[0, 3, 4, 7], [1, 2, 5, 6]]
GROUPS = [(0, 3), (1, 2)]

NEG_SLOPE = None


def _rv(x):
    return RuntimeValue(x)


def _binop(a, b, op):
    if isinstance(b, int):
        b = ex.make_scalar_value(b)
    return RuntimeValue(ex.binary_op(a, b, op))


class Sem:
    def __init__(self, ctx, nc, name):
        self.h = ctx.enter_context(nc.semaphore(name))
        self.n = 0

    def inc(self, instr, k=1):
        instr.then_inc(self.h, k)
        self.n += k
        return self.n

    def dma(self, instr):
        return self.inc(instr, 16)

    def wait(self, eng, val=None):
        eng.wait_ge(self.h, self.n if val is None else val)



def rope_tables():
    inv = 10000.0 ** (-np.arange(32) / 32.0)
    ang = np.arange(SEQ)[:, None] * inv[None, :]
    c = np.cos(ang).T.astype(np.float32)
    s = np.sin(ang).T.astype(np.float32)
    C64 = np.concatenate([c, c], axis=0)
    S64 = np.concatenate([s, -s], axis=0)
    C = np.concatenate([C64, C64], axis=0)
    Sm = np.concatenate([S64, S64], axis=0)
    return np.ascontiguousarray(C), np.ascontiguousarray(Sm)


def perm_l0_qk_cols(W):
    W = W.reshape(D, 16, 64)
    perm = np.concatenate([np.arange(0, 64, 2), np.arange(1, 64, 2)])
    W = W[:, :, perm]
    return np.ascontiguousarray(W.reshape(D, D))


def perm_l0_qk_bias(b):
    b = b.reshape(16, 64)
    perm = np.concatenate([np.arange(0, 64, 2), np.arange(1, 64, 2)])
    return np.ascontiguousarray(b[:, perm].reshape(D))


def chunk_feat(v):
    return np.ascontiguousarray(v.reshape(8, 128).T)


def host_prepare(inputs, nonce):
    f = lambda k: np.asarray(inputs[k], dtype=np.float32)
    i = lambda k: np.asarray(inputs[k], dtype=np.int32)
    x = f("L0_attn_x")

    C, Sm = rope_tables()
    tri = np.tril(np.ones((128, 128), dtype=np.float32)).T
    tri = np.ascontiguousarray(tri)

    shared = {
        "l0_wq": perm_l0_qk_cols(f("L0_attn_Wq")),
        "l0_wk": perm_l0_qk_cols(f("L0_attn_Wk")),
        "l0_wv": np.ascontiguousarray(f("L0_attn_Wv")),
        "l0_wo": np.ascontiguousarray(f("L0_attn_Wo")),
        "l0_bq": chunk_feat(perm_l0_qk_bias(f("L0_attn_bq"))),
        "l0_bk": chunk_feat(perm_l0_qk_bias(f("L0_attn_bk"))),
        "l0_bv": np.ascontiguousarray(f("L0_attn_bv").reshape(1, D)),
        "l0_bo": chunk_feat(f("L0_attn_bo")),
        "l0_gamma": chunk_feat(f("L0_attn_ln_gamma")),
        "l0_beta": chunk_feat(f("L0_attn_ln_beta")),
        "l0_wg": np.ascontiguousarray(i("L0_ffn_w_gate").T),
        "l0_wu": np.ascontiguousarray(i("L0_ffn_w_up").T),
        "l0_wd": np.ascontiguousarray(i("L0_ffn_w_down").T),
        "l1_wq": np.ascontiguousarray(f("L1_attn_Wq").T),
        "l1_wk": np.ascontiguousarray(f("L1_attn_Wk").T),
        "l1_wv": np.ascontiguousarray(f("L1_attn_Wv").T),
        "l1_wo": np.ascontiguousarray(f("L1_attn_Wo").T),
        "l1_su": np.ascontiguousarray(f("L1_ffn_shared_up").transpose(0, 2, 1)),
        "l1_sd": np.ascontiguousarray(f("L1_ffn_shared_down").transpose(0, 2, 1)),
        "l1_ru": np.ascontiguousarray(f("L1_ffn_routed_up").transpose(0, 2, 1)),
        "l1_rd": np.ascontiguousarray(f("L1_ffn_routed_down").transpose(0, 2, 1)),
        "l1_rw": np.ascontiguousarray(f("L1_ffn_router_w").T),
        "trimask": tri,
    }

    in_maps = []
    for c in range(8):
        seq, role = c // 2, c % 2
        chunks = ROLE_CHUNKS[role]
        xs = np.concatenate([x[seq, g * CH:(g + 1) * CH, :] for g in chunks], axis=0)
        pos = np.concatenate([np.arange(g * CH, (g + 1) * CH) for g in chunks])
        m = dict(shared)
        m["x"] = np.ascontiguousarray(xs.T)
        m["cl"] = np.ascontiguousarray(C[:, pos])
        m["sl"] = np.ascontiguousarray(Sm[:, pos])
        m["nonce"] = np.array([[nonce]], dtype=np.int32)
        in_maps.append(m)
    return in_maps


def host_assemble(results):
    y = np.empty((NB, SEQ, D), dtype=np.float32)
    for c in range(8):
        seq, role = c // 2, c % 2
        yc = results[c]["y"].T
        for li, g in enumerate(ROLE_CHUNKS[role]):
            y[seq, g * CH:(g + 1) * CH, :] = yc[li * CH:(li + 1) * CH, :]
    return y



class MK:

    def __init__(self, taps=(), stop_after="out", moe_waves=5, router_stage=4):
        self.taps = set(taps)
        self.stop_after = stop_after
        self.moe_waves = moe_waves
        self.router_stage = router_stage
        self.nc = bass.Bass("TRN2", target_bir_lowering=False, detect_race_conditions=False)
        self.ctx = contextlib.ExitStack()
        self.tap_outs = {}

    def _ctx(self):
        c = contextlib.ExitStack()
        if not hasattr(self, "live"):
            self.live = []
        self.live.append(c)
        return c

    def _close(self, c):
        assert self.live and self.live[-1] is c, "phase ctx close out of order"
        self.live.pop()
        c.close()

    def _close_all(self):
        for c in reversed(getattr(self, "live", [])):
            c.close()
        self.live = []

    def sem(self, name):
        s = Sem(self.ctx, self.nc, name)
        if not hasattr(self, "all_sems"):
            self.all_sems = []
        self.all_sems.append(s)
        return s

    def sb(self, name, shape, dt, ctx=None):
        self._uid = getattr(self, "_uid", 0) + 1
        return (ctx or self.ctx).enter_context(
            self.nc.sbuf_tensor(f"{name}_{self._uid}", shape, dt))

    def ps(self, name, shape, dt=F32, ctx=None):
        self._uid = getattr(self, "_uid", 0) + 1
        return (ctx or self.ctx).enter_context(
            self.nc.psum_tensor(f"{name}_{self._uid}", shape, dt))

    def ln_scratch(self, ctx):
        self.tmp_sb = self.sb("tmp_sb", [128, TLOC], F32, ctx=ctx)
        self.tmp2_sb = self.sb("tmp2_sb", [128, TLOC], F32, ctx=ctx)
        self.s1_sb = self.sb("s1_sb", [1, TLOC], F32, ctx=ctx).ap()
        self.s2_sb = self.sb("s2_sb", [1, TLOC], F32, ctx=ctx).ap()
        self.sc_sb = self.sb("sc_sb", [1, TLOC], F32, ctx=ctx).ap()
        self.sc2_sb = self.sb("sc2_sb", [1, TLOC], F32, ctx=ctx).ap()
        self.mu_sb = self.sb("mu_sb", [1, TLOC], F32, ctx=ctx).ap()
        self.a1_sb = self.sb("a1_sb", [1, TLOC], F32, ctx=ctx).ap()
        self.aa_sb = self.sb("aa_sb", [1, TLOC], F32, ctx=ctx).ap()

    def xbf_ap(self):
        return self.tmp_sb.ap().bitcast(BF16)[:, 0:TLOC]

    def sqbf_ap(self):
        return self.tmp2_sb.ap().bitcast(BF16)[:, 0:TLOC]

    def tap(self, name, ap_fn, shape, dt=F32):
        if name in self.taps:
            d = self.nc.dram_tensor("tap_" + name, shape, dt, kind="ExternalOutput").ap()
            self.tap_outs[name] = (d, ap_fn)

    def build(self):
        nc = self.nc
        self.x_d = nc.dram_tensor("x", [D, TLOC], F32, kind="ExternalInput").ap()
        self.cl_d = nc.dram_tensor("cl", [128, TLOC], F32, kind="ExternalInput").ap()
        self.sl_d = nc.dram_tensor("sl", [128, TLOC], F32, kind="ExternalInput").ap()
        self.tri_d = nc.dram_tensor("trimask", [128, 128], F32, kind="ExternalInput").ap()
        self.nonce_d = nc.dram_tensor("nonce", [1, 1], I32, kind="ExternalInput").ap()
        self.w_d = {}
        for n, shp in [
            ("l0_wq", [D, D]), ("l0_wk", [D, D]), ("l0_wv", [D, D]), ("l0_wo", [D, D]),
            ("l1_wq", [D, D]), ("l1_wk", [D, D]), ("l1_wv", [D, D]), ("l1_wo", [D, D]),
            ("l0_bq", [128, 8]), ("l0_bk", [128, 8]), ("l0_bo", [128, 8]),
            ("l0_bv", [1, D]), ("l0_gamma", [128, 8]), ("l0_beta", [128, 8]),
            ("l1_su", [2, D, 512]), ("l1_sd", [2, 512, D]),
            ("l1_ru", [16, D, 512]), ("l1_rd", [16, 512, D]), ("l1_rw", [D, 16]),
        ]:
            self.w_d[n] = nc.dram_tensor(n, shp, F32, kind="ExternalInput").ap()
        for n, shp in [("l0_wg", [D, HID]), ("l0_wu", [D, HID]), ("l0_wd", [HID, D])]:
            self.w_d[n] = nc.dram_tensor(n, shp, I32, kind="ExternalInput").ap()
        self.y_d = nc.dram_tensor("y", [D, TLOC], F32, kind="ExternalOutput").ap()
        self.shk_d = [nc.dram_tensor(f"shk{l}", [NCH, NHP, 128, CH], BF16,
                                     kind="Internal", addr_space="Shared").ap() for l in range(2)]
        self.shv_d = [nc.dram_tensor(f"shv{l}", [NCH, NHP, 4, 128, 130], BF16,
                                     kind="Internal", addr_space="Shared").ap() for l in range(2)]
        self.shflag_d = [nc.dram_tensor(f"shflag{l}", [2, 1, 4], I32,
                                        kind="Internal", addr_space="Shared").ap() for l in range(2)]
        self.shnorm_d = [nc.dram_tensor(f"shnorm{l}", [2, 1, 4], F32,
                                        kind="Internal", addr_space="Shared").ap() for l in range(2)]

        c = self.ctx
        nc_ = nc
        self.x_sb = self.sb("x_sb", [128, 8, TLOC], F32)
        self.xn_sb = self.sb("xn_sb", [128, 8, TLOC], BF16)
        self.tri_sb = self.sb("tri_sb", [128, 128], BF16)
        self.ident = self.sb("ident", [128, 128], BF16)
        self.ident_f = self.sb("ident_f", [128, 128], F32)
        self.onescol_bf = self.sb("onescol_bf", [128, 1], BF16)
        self.onesrow_f = self.sb("onesrow_f", [1, 128], F32)
        self.onesrow_bf = self.sb("onesrow_bf", [1, 128], BF16)
        self.prm = {}
        for n in ["l0_bq", "l0_bk", "l0_bo", "l0_gamma", "l0_beta"]:
            self.prm[n] = self.sb(n + "_sb", [128, 8], F32)
        self.prm["l0_bv"] = self.sb("l0_bv_sb", [1, D], BF16)
        self.nonce_sb = self.sb("nonce_sb", [1, 4], I32)
        self.flag_sb = self.sb("flag_sb", [1, 4], I32)
        self.pflag_sb = self.sb("pflag_sb", [1, 4], I32)
        self.mynorm_sb = self.sb("mynorm_sb", [1, 4], F32)
        self.pnorm_sb = self.sb("pnorm_sb", [1, 4], F32)
        self.negb_sb = self.sb("negb_sb", [128, 1], F32)
        self.bscr_sb = self.sb("bscr_sb", [1, 8], F32)
        self.normc_sb = self.sb("normc_sb", [128, 4], F32)
        for n in ["sLD", "sW", "sT", "sV", "sSc", "sG", "sPol", "sPr", "sY",
                  "sKW", "sFl", "sEv", "sRC", "sBC", "sQN"]:
            setattr(self, n, self.sem(n))
        self.aQ = [self.sem(f"aQ{i}") for i in range(2)]
        self.aK = [self.sem(f"aK{i}") for i in range(2)]
        self.kW = [self.sem(f"kW{i}") for i in range(2)]
        self.vW = [self.sem(f"vW{i}") for i in range(2)]
        self.stK = [self.sem(f"stK{i}") for i in range(2)]
        self.wF = [self.sem(f"wF{i}") for i in range(2)]

        self.block = c.enter_context(nc.Block())
        self.emit_init()
        order = ["ln_a0", "qkv0", "xchg0", "attn0", "wo0", "ln_b0", "ffn",
                 "ln_a1", "qkv1", "xchg1", "attn1", "wo1", "ln_b1", "moe", "out"]
        if self.stop_after == "router":
            order[order.index("moe")] = "moe"
            order = order[:order.index("moe") + 1]
        self.xdep = (self.sLD, self.sLD.n)
        for ph in order:
            l = int(ph[-1]) if ph[-1] in "01" else None
            kind = ph.rstrip("01")
            if kind == "ln_a":
                self.emit_ln_a(l)
            elif kind == "qkv":
                self.emit_qkv(l)
            elif kind == "xchg":
                self.emit_xchg(l)
            elif kind == "attn":
                self.emit_attn(l)
            elif kind == "wo":
                self.emit_wo(l)
            elif kind == "ln_b":
                self.emit_ln_b(l)
            elif kind == "ffn":
                self.emit_ffn()
            elif kind == "moe":
                self.emit_router()
                if self.stop_after != "router":
                    self.emit_moe()
            elif kind == "out":
                self.emit_out()
            if ph == self.stop_after:
                break
        self.emit_taps()
        self._close_all()
        self.ctx.close()
        return self.nc

    def emit_init(self):
        b = self.block

        @b.sync
        def _(s):
            self.sLD.dma(s.dma_start(out=self.x_sb[:],
                                     in_=self.x_d.rearrange("(c p) t -> p c t", p=128)))
            for n in ["l0_bq", "l0_bk", "l0_bo", "l0_gamma", "l0_beta"]:
                self.sLD.dma(s.dma_start(out=self.prm[n][:], in_=self.w_d[n][:, :]))
            self.sLD.dma(s.dma_start(out=self.nonce_sb[:, 0:1], in_=self.nonce_d[:, :]))

        @b.gpsimd
        def _(g):
            self.sW.dma(g.dma_start(out=self.tri_sb[:], in_=self.tri_d[:, :]))
            self.sW.dma(g.dma_start(out=self.prm["l0_bv"][:], in_=self.w_d["l0_bv"][:, :]))
            g.iota(self.ident[:], pattern=[[1, 128]], base=0, channel_multiplier=-1,
                   allow_small_or_imprecise_dtypes=True).then_inc(self.sG.h, 1)
            g.iota(self.ident_f[:], pattern=[[1, 128]], base=0, channel_multiplier=-1,
                   allow_small_or_imprecise_dtypes=True).then_inc(self.sG.h, 1)
            g.memset(self.onescol_bf[:], 1.0).then_inc(self.sG.h, 1)
            g.memset(self.onesrow_f[:], 1.0).then_inc(self.sG.h, 1)
            g.memset(self.onesrow_bf[:], 1.0).then_inc(self.sG.h, 1)
            g.memset(self.mynorm_sb[:], 0.0)
            g.memset(self.pnorm_sb[:], 0.0)
            g.memset(self.bscr_sb[:], 0.0)
            g.memset(self.negb_sb[:], 0.0)
            g.memset(self.flag_sb[:], 0)
            g.memset(self.pflag_sb[:], 0).then_inc(self.sG.h, 1)
            self.role_g = g.to_reg(_binop(g.partition_id(), 2, ALU.mod))
            self.sG.n += 6

        @b.tensor
        def _(t):
            self.role_t = t.to_reg(_binop(t.partition_id(), 2, ALU.mod))

        self.init_gate = self.sG.n

        @b.vector
        def _(v):
            self.sG.wait(v)
            v.tensor_scalar(out=self.ident[:], in0=self.ident[:], scalar1=0.0,
                            scalar2=None, op0=ALU.is_equal)
            ins = v.tensor_scalar(out=self.ident_f[:], in0=self.ident_f[:], scalar1=0.0,
                                  scalar2=None, op0=ALU.is_equal)
            self.sV.inc(ins)

    def _stats(self, l, xdep, want_aa):
        b = self.block
        pctx = self._ctx()
        s1p = self.ps("s1p", [1, TLOC], ctx=pctx)
        s2p = self.ps("s2p", [1, TLOC], ctx=pctx)
        vbase = self.sV.n
        scbase = self.sSc.n

        @b.vector
        def _(v):
            for c in range(8):
                if c == 0:
                    xdep[0].wait(v, xdep[1])
                self.sV.inc(v.tensor_copy(self.xbf_ap(), self.x_sb[:, c, :]))
                self.sV.inc(v.tensor_mul(self.sqbf_ap(), self.xbf_ap(), self.xbf_ap()))
                if c < 7:
                    self.sT.wait(v, self.sT.n + (c + 1) * 8)

        @b.tensor
        def _(t):
            t.wait_ge(self.sG.h, self.init_gate)
            for c in range(8):
                self.sV.wait(t, vbase + 2 * c + 1)
                for tt in range(4):
                    t.matmul(s1p[0:1, tt * 512:(tt + 1) * 512], self.onescol_bf[:],
                             self.xbf_ap()[:, tt * 512:(tt + 1) * 512],
                             start=(c == 0), stop=(c == 7))
                self.sV.wait(t, vbase + 2 * c + 2)
                for tt in range(4):
                    ins = t.matmul(s2p[0:1, tt * 512:(tt + 1) * 512], self.onescol_bf[:],
                                   self.sqbf_ap()[:, tt * 512:(tt + 1) * 512],
                                   start=(c == 0), stop=(c == 7))
                ins.then_inc(self.sT.h, 8)
                self.sT.n += 8

        tdone = self.sT.n

        @b.vector
        def _(v):
            self.sT.wait(v, tdone)
            v.tensor_copy(self.s1_sb[:], s1p[:, :])
            v.tensor_copy(self.s2_sb[:], s2p[:, :])
            v.tensor_scalar_mul(self.mu_sb[:], self.s1_sb[:], 1.0 / D)
            v.tensor_scalar_mul(self.sc_sb[:], self.s2_sb[:], 1.0 / D)
            v.tensor_mul(self.sc2_sb[:], self.mu_sb[:], self.mu_sb[:])
            v.tensor_sub(self.sc_sb[:], self.sc_sb[:], self.sc2_sb[:])
            self.sV.inc(v.tensor_scalar_add(self.sc2_sb[:, :], self.sc_sb[:], EPS))

        @b.scalar
        def _(s):
            self.sV.wait(s)
            self.sSc.inc(s.activation(out=self.a1_sb[:], in_=self.sc2_sb[:], func=AF.Sqrt))

        @b.vector
        def _(v):
            self.sSc.wait(v)
            self.sV.inc(v.reciprocal(self.a1_sb[:], self.a1_sb[:]))

        if want_aa:
            @b.vector
            def _(v):
                v.reciprocal(self.aa_sb[:], self.sc2_sb[:])
                v.tensor_mul(self.aa_sb[:], self.aa_sb[:], self.sc_sb[:])
                self.sV.inc(v.tensor_scalar_add(self.aa_sb[:], self.aa_sb[:], EPS))

            @b.scalar
            def _(s):
                self.sV.wait(s)
                self.sSc.inc(s.activation(out=self.aa_sb[:], in_=self.aa_sb[:], func=AF.Sqrt))

            @b.vector
            def _(v):
                self.sSc.wait(v)
                v.reciprocal(self.aa_sb[:], self.aa_sb[:])
                self.sV.inc(v.tensor_mul(self.aa_sb[:], self.aa_sb[:], self.a1_sb[:]))
        self._close(pctx)

    def _bcast(self, dst_ps, src_sb, dep_sem, dep_val):
        b = self.block

        @b.tensor
        def _(t):
            dep_sem.wait(t, dep_val)
            for tt in range(4):
                ins = t.matmul(dst_ps[:, tt * 512:(tt + 1) * 512], self.onesrow_f[:, :],
                               src_sb[:, tt * 512:(tt + 1) * 512], start=True, stop=True)
            self.sBC.inc(ins)

    def emit_ln_a(self, l):
        b = self.block
        pctx = self._ctx()
        self.ln_scratch(pctx)
        self._stats(l, self.xdep, want_aa=(l == 0))
        b1 = self.ps("lnb1", [128, TLOC], ctx=pctx)
        b2 = self.ps("lnb2", [128, TLOC], ctx=pctx)
        self._bcast(b1, self.mu_sb, self.sV, self.sV.n)
        self._bcast(b2, self.aa_sb if l == 0 else self.a1_sb, self.sV, self.sV.n)
        bc1 = self.sBC.n

        if l == 0:
            @b.vector
            def _(v):
                self.sBC.wait(v, bc1)
                for c in range(8):
                    v.tensor_sub(self.tmp_sb[:], self.x_sb[:, c, :], b1[:, :])
                    v.tensor_mul(self.tmp2_sb[:], self.tmp_sb[:], b2[:, :])
                    ins = v.tensor_scalar(out=self.xn_sb[:, c, :], in0=self.tmp2_sb[:],
                                          scalar1=self.prm["l0_gamma"][:, c:c + 1],
                                          scalar2=self.prm["l0_beta"][:, c:c + 1],
                                          op0=ALU.mult, op1=ALU.add)
                self.sV.inc(ins)
            self._bcast(b2, self.a1_sb, self.sV, self.sV.n)
            bc2 = self.sBC.n

            @b.vector
            def _(v):
                self.sBC.wait(v, bc2)
                for c in range(8):
                    v.tensor_sub(self.tmp_sb[:], self.x_sb[:, c, :], b1[:, :])
                    v.tensor_mul(self.tmp2_sb[:], self.tmp_sb[:], b2[:, :])
                    ins = v.tensor_add(self.x_sb[:, c, :], self.x_sb[:, c, :], self.tmp2_sb[:])
                self.sV.inc(ins)
        else:
            @b.vector
            def _(v):
                self.sBC.wait(v, bc1)
                for c in range(8):
                    v.tensor_sub(self.tmp_sb[:], self.x_sb[:, c, :], b1[:, :])
                    ins = v.tensor_mul(self.xn_sb[:, c, :], self.tmp_sb[:], b2[:, :])
                self.sV.inc(ins)
        self._close(pctx)
        self.tap(f"xn{l}", lambda: self.xn_sb.ap(), [128, 8, TLOC], BF16)
        self.tap(f"xr{l}", lambda: self.x_sb.ap(), [128, 8, TLOC], F32)

    def emit_ln_b(self, l):
        self.xdep = (self.sV, self.sV.n)
        b = self.block
        pctx = self._ctx()
        self.ln_scratch(pctx)
        self._stats(l, self.xdep, want_aa=False)
        b1 = self.ps("lnb1b", [128, TLOC], ctx=pctx)
        b2 = self.ps("lnb2b", [128, TLOC], ctx=pctx)
        self._bcast(b1, self.mu_sb, self.sV, self.sV.n)
        self._bcast(b2, self.a1_sb, self.sV, self.sV.n)
        bc = self.sBC.n

        @b.vector
        def _(v):
            self.sBC.wait(v, bc)
            for c in range(8):
                v.tensor_sub(self.tmp_sb[:], self.x_sb[:, c, :], b1[:, :])
                ins = v.tensor_mul(self.xn_sb[:, c, :], self.tmp_sb[:], b2[:, :])
            self.sV.inc(ins)
        self._close(pctx)
        self.tap(f"xnb{l}", lambda: self.xn_sb.ap(), [128, 8, TLOC], BF16)

    def emit_qkv(self, l):
        b = self.block
        lyr = f"l{l}"
        pctx = self._ctx()
        self.qo_sb = self.sb("qo_sb", [128, NHP, TLOC], BF16, ctx=pctx)
        qsctx = self._ctx()
        cl_sb = self.sb("cl", [128, TLOC], F32, ctx=qsctx)
        sl_sb = self.sb("sl", [128, TLOC], F32, ctx=qsctx)
        wA = self.sb("wA", [128, 8, D], BF16, ctx=qsctx)
        kstage = self.sb("kstage", [128, 2, CH], BF16, ctx=qsctx)
        ropeb = self.sb("ropeb", [128, 2, CH], BF16, ctx=qsctx)
        ksq = self.sb("ksq", [128, 2, CH], BF16, ctx=qsctx)
        vstage = self.sb("vstage", [128, 2, 8, 130], BF16, ctx=qsctx)
        nrmk_sb = self.sb("nrmk", [1, 64], F32, ctx=qsctx)
        qpsctx = self._ctx()
        qkp = self.ps("qkp", [128, 2, 512], ctx=qpsctx)
        vp = self.ps("vp", [128, 2, 1024], ctx=qpsctx)
        nrmp = self.ps("nrmp", [1, 2, 512], ctx=qpsctx)

        sQP = self.sem(f"sQP{l}")
        sRP = self.sem(f"sRP{l}")
        sSQ = self.sem(f"sSQ{l}")
        sNM = self.sem(f"sNM{l}")
        sNR = self.sem(f"sNR{l}")
        sVP = self.sem(f"sVP{l}")
        sVE = self.sem(f"sVE{l}")

        xn_ready = self.sV.n

        @b.gpsimd
        def _(g):
            g.wait_ge(self.sV.h, xn_ready)
            self.sW.dma(g.dma_start(out=wA[:], in_=self.w_d[lyr + "_wq"].rearrange("(c p) o -> p c o", p=128)))
        w_q_ready = self.sW.n

        @b.sync
        def _(s):
            s.wait_ge(self.sV.h, xn_ready)
            self.sLD.dma(s.dma_start(out=cl_sb[:], in_=self.cl_d[:, :]))
            self.sLD.dma(s.dma_start(out=sl_sb[:], in_=self.sl_d[:, :]))
        ld_ready = self.sLD.n

        for which in (0, 1):
            wsb = wA
            bias = self.prm["l0_bq" if which == 0 else "l0_bk"] if l == 0 else None
            aS = self.aQ if which == 0 else self.aK
            abase = [aS[0].n, aS[1].n]
            kwbase = [self.kW[0].n, self.kW[1].n]
            qp0, rp0, sq0, nm0, nr0 = sQP.n, sRP.n, sSQ.n, sNM.n, sNR.n

            wrdy = w_q_ready if which == 0 else self.w_k_ready

            @b.tensor
            def _(t, wsb=wsb, which=which, qp0=qp0, rp0=rp0, sq0=sq0, nm0=nm0,
                  nr0=nr0, wrdy=wrdy):
                t.wait_ge(self.sW.h, wrdy)
                t.wait_ge(self.sV.h, xn_ready)
                t.wait_ge(self.sLD.h, ld_ready)
                for k in range(32):
                    hp, tt = k // 4, k % 4
                    slot = k % 2
                    if rp0 + k - 1 > 0:
                        t.wait_ge(sRP.h, rp0 + k - 1)
                    for dc in range(8):
                        ins = t.matmul(qkp[:, slot, :], wsb[:, dc, hp * 128:(hp + 1) * 128],
                                       self.xn_sb[:, dc, tt * 512:(tt + 1) * 512],
                                       start=(dc == 0), stop=(dc == 7))
                    ins.then_inc(sQP.h, 1)
                    if k >= 1:
                        j = k - 1
                        t.wait_ge(sSQ.h, sq0 + j + 1)
                        if j >= 2:
                            t.wait_ge(sNR.h, nr0 + j - 1)
                        t.matmul(nrmp[0:1, j % 2, :], self.onescol_bf[:], ksq[:, j % 2, :],
                                 start=True, stop=True).then_inc(sNM.h, 1)
                j = 31
                t.wait_ge(sSQ.h, sq0 + j + 1)
                t.wait_ge(sNR.h, nr0 + j - 1)
                t.matmul(nrmp[0:1, j % 2, :], self.onescol_bf[:], ksq[:, j % 2, :],
                         start=True, stop=True).then_inc(sNM.h, 1)
            sQP.n = qp0 + 32
            sNM.n = nm0 + 32

            @b.vector
            def _(v, which=which, bias=bias, aS=aS, abase=abase, kwbase=kwbase,
                  qp0=qp0, sq0=sq0, nm0=nm0, ckstage=kstage):
                for k in range(32):
                    hp, tt = k // 4, k % 4
                    slot = k % 2
                    v.wait_ge(sQP.h, qp0 + k + 1)
                    if k >= 2:
                        v.wait_ge(aS[slot].h, abase[slot] + 64 * (k // 2))
                    elif which == 1:
                        v.wait_ge(self.aQ[slot].h, self.aQ[slot].n)
                    if nm0 + k - 1 > 0:
                        v.wait_ge(sNM.h, nm0 + k - 1)
                    destA = (self.qo_sb[:, hp, tt * 512:(tt + 1) * 512] if which == 0
                             else ckstage[:, slot, :])
                    if which == 1 and k >= 2:
                        v.wait_ge(self.kW[slot].h, kwbase[slot] + 16 * (k // 2))
                    bias_ap = bias[:, hp:hp + 1] if bias is not None else 0.0
                    v.scalar_tensor_tensor(out=destA, in0=qkp[:, slot, :], scalar=bias_ap,
                                           in1=cl_sb[:, tt * 512:(tt + 1) * 512],
                                           op0=ALU.add, op1=ALU.mult)
                    v.scalar_tensor_tensor(out=ropeb[:, slot, :], in0=qkp[:, slot, :],
                                           scalar=bias_ap,
                                           in1=sl_sb[:, tt * 512:(tt + 1) * 512],
                                           op0=ALU.add, op1=ALU.mult).then_inc(sRP.h, 1)
                    v.wait_ge(aS[slot].h, abase[slot] + 64 * (k // 2 + 1))
                    v.tensor_mul(ksq[:, slot, :], destA, destA).then_inc(sSQ.h, 1)
                    if k >= 1:
                        j = k - 1
                        v.wait_ge(sNM.h, nm0 + j + 1)
                        v.reduce_max(nrmk_sb[0:1, which * 32 + j:which * 32 + j + 1],
                                     nrmp[0:1, j % 2, :], axis=mybir.AxisListType.X).then_inc(sNR.h, 1)
                j = 31
                v.wait_ge(sNM.h, nm0 + 32)
                v.reduce_max(nrmk_sb[0:1, which * 32 + j:which * 32 + j + 1],
                             nrmp[0:1, j % 2, :], axis=mybir.AxisListType.X).then_inc(sNR.h, 1)
            sRP.n = rp0 + 32
            sSQ.n = sq0 + 32
            sNR.n = nr0 + 32

            @b.gpsimd
            def _(g, which=which, aS=aS, abase=abase, rp0=rp0, kwbase=kwbase,
                  ckstage=kstage):
                for k in range(32):
                    hp, tt = k // 4, k % 4
                    slot = k % 2
                    g.wait_ge(sRP.h, rp0 + k + 1)
                    destA = (self.qo_sb[:, hp, tt * 512:(tt + 1) * 512] if which == 0
                             else ckstage[:, slot, :])
                    for lo in (0, 64):
                        g.dma_start(out=destA[lo:lo + 32, :], in_=ropeb[lo + 32:lo + 64, slot, :],
                                    accum_op=ALU.add).then_inc(aS[slot].h, 16)
                        g.dma_start(out=destA[lo + 32:lo + 64, :], in_=ropeb[lo:lo + 32, slot, :],
                                    accum_op=ALU.add).then_inc(aS[slot].h, 16)
                    if which == 1:
                        g.wait_ge(aS[slot].h, abase[slot] + 64 * (k // 2 + 1))
                        with g.If_eq(self.role_g, 0):
                            g.dma_start(out=self.shk_d[l][ROLE_CHUNKS[0][tt], hp, :, :],
                                        in_=ckstage[:, slot, :]).then_inc(self.kW[slot].h, 16)
                        with g.Else():
                            g.dma_start(out=self.shk_d[l][ROLE_CHUNKS[1][tt], hp, :, :],
                                        in_=ckstage[:, slot, :]).then_inc(self.kW[slot].h, 16)
                        self.kW[slot].n += 16
            aS[0].n = abase[0] + 64 * 16
            aS[1].n = abase[1] + 64 * 16


            if which == 0:
                @b.gpsimd
                def _(g, qp0=qp0):
                    g.wait_ge(sQP.h, qp0 + 32)
                    self.sW.dma(g.dma_start(out=wA[:], in_=self.w_d[lyr + "_wk"].rearrange("(c p) o -> p c o", p=128)))
                self.w_k_ready = self.sW.n

        @b.gpsimd
        def _(g):
            g.wait_ge(sQP.h, 64)
            self.sW.dma(g.dma_start(out=wA[:], in_=self.w_d[lyr + "_wv"].rearrange("(c p) o -> p c o", p=128)))
        w_v_ready = self.sW.n

        vp0, ve0 = sVP.n, sVE.n
        vwbase = [self.vW[0].n, self.vW[1].n]

        @b.vector
        def _(v):
            v.memset(vstage[:, :, :, 64:65], 1.0)
            v.memset(vstage[:, :, :, 129:130], 1.0).then_inc(sVE.h, 1)

        @b.tensor
        def _(t):
            t.wait_ge(self.sW.h, w_v_ready)
            t.wait_ge(self.sG.h, self.init_gate)
            for k in range(16):
                slot = k % 2
                if k >= 2:
                    t.wait_ge(sVE.h, ve0 + 1 + (k - 2) + 1)
                for dc in range(8):
                    t.matmul(vp[:, slot, 0:512], self.xn_sb[:, dc, k * 128:(k + 1) * 128],
                             wA[:, dc, 0:512], start=(dc == 0), stop=(dc == 7 and l != 0))
                    ins = t.matmul(vp[:, slot, 512:1024], self.xn_sb[:, dc, k * 128:(k + 1) * 128],
                                   wA[:, dc, 512:1024], start=(dc == 0), stop=(dc == 7 and l != 0))
                if l == 0:
                    t.matmul(vp[:, slot, 0:512], self.onesrow_bf[:, :],
                             self.prm["l0_bv"][0:1, 0:512], start=False, stop=True)
                    ins = t.matmul(vp[:, slot, 512:1024], self.onesrow_bf[:, :],
                                   self.prm["l0_bv"][0:1, 512:1024], start=False, stop=True)
                ins.then_inc(sVP.h, 1)
        sVP.n = vp0 + 16

        @b.vector
        def _(v):
            for k in range(16):
                slot = k % 2
                v.wait_ge(sVP.h, vp0 + k + 1)
                if k >= 2:
                    v.wait_ge(self.vW[slot].h, vwbase[slot] + 16 * (k // 2))
                src = vp.ap()[:, slot, :].rearrange("p (h t d) -> p h t d", h=8, t=2)
                v.tensor_copy(vstage[:, slot, :, 0:64], src[:, :, 0, :])
                v.tensor_copy(vstage[:, slot, :, 65:129], src[:, :, 1, :]).then_inc(sVE.h, 1)
        sVE.n = ve0 + 17

        @b.gpsimd
        def _(g):
            for k in range(16):
                slot = k % 2
                lc, kt = k // 4, k % 4
                g.wait_ge(sVE.h, ve0 + 1 + k + 1)
                with g.If_eq(self.role_g, 0):
                    g.dma_start(out=self.shv_d[l][ROLE_CHUNKS[0][lc], :, kt, :, :].transpose([1, 0, 2]),
                                in_=vstage[:, slot, :, :]).then_inc(self.vW[slot].h, 16)
                with g.Else():
                    g.dma_start(out=self.shv_d[l][ROLE_CHUNKS[1][lc], :, kt, :, :].transpose([1, 0, 2]),
                                in_=vstage[:, slot, :, :]).then_inc(self.vW[slot].h, 16)
                self.vW[slot].n += 16

        @b.vector
        def _(v):
            v.wait_ge(sNR.h, sNR.n)
            v.reduce_max(self.mynorm_sb[0:1, 1:2], nrmk_sb[0:1, 32:64], axis=mybir.AxisListType.X)
            self.sV.inc(v.reduce_max(self.mynorm_sb[0:1, 0:1], nrmk_sb[0:1, 0:32], axis=mybir.AxisListType.X))
        self._close(qpsctx)
        self._close(qsctx)
        self.pctx_qkv = pctx
        self.tap(f"q{l}", lambda: self.qo_sb.ap(), [128, NHP, TLOC], BF16)
        self.tap(f"norm{l}", lambda: self.mynorm_sb.ap(), [1, 4], F32)

    @staticmethod
    def attn_passes(role, g):
        chunks = ROLE_CHUNKS[role]
        la, lb = GROUPS[g]
        a, bg = chunks[la], chunks[lb]
        out = [("D", la, a, 0), ("D", lb, bg, 1)]
        for kg in range(a):
            out.append(("O", la, kg, 0))
        for kg in range(bg):
            out.append(("O", lb, kg, 1))
        assert len(out) == 9
        return out

    def emit_xchg(self, l):
        b = self.block
        norms_done = self.sV.n

        @b.gpsimd
        def _(g):
            for s in (self.kW + self.vW + self.aQ + self.aK):
                g.wait_ge(s.h, s.n)
            self.sG.wait(g)
            g.wait_ge(self.sV.h, norms_done)
            with g.If_eq(self.role_g, 0):
                g.dma_start(out=self.shnorm_d[l][0, :, 0:2], in_=self.mynorm_sb[:, 0:2]).then_inc(self.sFl.h, 16)
            with g.Else():
                g.dma_start(out=self.shnorm_d[l][1, :, 0:2], in_=self.mynorm_sb[:, 0:2]).then_inc(self.sFl.h, 16)
            self.sFl.n += 16
            self.sFl.wait(g)
            if not hasattr(self, "fl_val"):
                self.fl_val = g.alloc_register("fl_val")
                self.fl_cur = g.alloc_register("fl_cur")
                self.fl_cnt = g.alloc_register("fl_cnt")
                g.reg_mov(self.fl_cnt, 0)
            g.load(self.fl_val, self.nonce_sb[0:1, 0:1])
            g.reg_add(self.fl_val, self.fl_val, l + 1)
            g.store(self.flag_sb[0:1, 0:1], self.fl_val)
            with g.If_eq(self.role_g, 0):
                g.dma_start(out=self.shflag_d[l][0, :, 0:1], in_=self.flag_sb[0:1, 0:1]).then_inc(self.sFl.h, 16)
            with g.Else():
                g.dma_start(out=self.shflag_d[l][1, :, 0:1], in_=self.flag_sb[0:1, 0:1]).then_inc(self.sFl.h, 16)
            self.sFl.n += 16
            g.reg_mov(self.fl_cur, 0)
            with g.If_eq(self.role_g, 0):
                with g.While(lambda: _rv(ex.binary_op(_rv(self.fl_cur), _rv(self.fl_val), ALU.not_equal))):
                    g.reg_add(self.fl_cnt, self.fl_cnt, 16)
                    g.dma_start(out=self.pflag_sb[0:1, 0:1], in_=self.shflag_d[l][1, :, 0:1]).then_inc(self.sPol.h, 16)
                    g.wait_ge(self.sPol.h, self.fl_cnt)
                    g.load(self.fl_cur, self.pflag_sb[0:1, 0:1])
                g.dma_start(out=self.pnorm_sb[:, 0:2], in_=self.shnorm_d[l][1, :, 0:2]).then_inc(self.sPr.h, 16)
            with g.Else():
                with g.While(lambda: _rv(ex.binary_op(_rv(self.fl_cur), _rv(self.fl_val), ALU.not_equal))):
                    g.reg_add(self.fl_cnt, self.fl_cnt, 16)
                    g.dma_start(out=self.pflag_sb[0:1, 0:1], in_=self.shflag_d[l][0, :, 0:1]).then_inc(self.sPol.h, 16)
                    g.wait_ge(self.sPol.h, self.fl_cnt)
                    g.load(self.fl_cur, self.pflag_sb[0:1, 0:1])
                g.dma_start(out=self.pnorm_sb[:, 0:2], in_=self.shnorm_d[l][0, :, 0:2]).then_inc(self.sPr.h, 16)
            self.sPr.n += 16
            g.wait_ge(self.sPr.h, self.sPr.n)
            g.nop().then_inc(self.sG.h, 1)
            self.sG.n += 1
        partner_ready = self.sG.n

        pctx = self._ctx()
        bp = self.ps("bp", [128, 1], ctx=pctx)

        @b.vector
        def _(v):
            g = None
            v.wait_ge(self.sG.h, partner_ready)
            v.tensor_max(self.bscr_sb[0:1, 0:1], self.mynorm_sb[0:1, 1:2], self.pnorm_sb[0:1, 1:2])
            self.sV.inc(v.tensor_mul(self.bscr_sb[0:1, 1:2], self.bscr_sb[0:1, 0:1], self.mynorm_sb[0:1, 0:1]))

        @b.scalar
        def _(s):
            self.sV.wait(s)
            self.sSc.inc(s.activation(out=self.bscr_sb[0:1, 2:3], in_=self.bscr_sb[0:1, 1:2], func=AF.Sqrt))

        @b.vector
        def _(v):
            self.sSc.wait(v)
            self.sV.inc(v.tensor_scalar_mul(self.bscr_sb[0:1, 3:4], self.bscr_sb[0:1, 2:3], -0.127551))

        @b.tensor
        def _(t):
            self.sV.wait(t)
            t.wait_ge(self.sG.h, self.init_gate)
            self.sT.inc(t.matmul(bp[:, :], self.onesrow_f[:, :], self.bscr_sb[0:1, 3:4], start=True, stop=True))

        @b.vector
        def _(v):
            self.sT.wait(v)
            self.sV.inc(v.tensor_copy(self.negb_sb[:], bp[:, :]))
        self._close(pctx)
        self.b_ready = self.sV.n

    UNIT_ORDER = [(0, 0), (1, 0), (0, 1), (1, 1)]

    def emit_attn(self, l):
        b = self.block
        pctx0 = self._ctx()
        woctx = pctx0
        pctx = self._ctx()
        psctx = self._ctx()
        self.pctx_wo = woctx
        wo_sb = self.sb("wo", [128, 8, D], BF16, ctx=woctx)
        self.wo_sb = wo_sb
        kstg = self.sb("kstg", [128, 2, CH], BF16, ctx=pctx)
        vstg = self.sb("vstg", [128, 2, 4, 130], BF16, ctx=pctx)
        pt_sb = self.sb("pt", [128, 2, 1024], BF16, ctx=pctx)
        rc_sb = self.sb("rc", [1, 2048], F32, ctx=pctx)
        bcast_sb = self.sb("bcast", [64, 2048], F32, ctx=pctx)
        st = self.ps("st", [128, 2, 1024], ctx=psctx)
        acc = [self.ps(f"acc{h}", [65, 1024], ctx=psctx) for h in range(2)]
        self.pctx_attn = pctx
        self.psctx_attn = psctx

        sIn = self.sem(f"sIn{l}")
        stS = [self.sem(f"stS{l}_{i}") for i in range(2)]
        uQK = self.sem(f"uQK{l}")
        uEX = self.sem(f"uEX{l}")
        uMK = self.sem(f"uMK{l}")
        uPV = self.sem(f"uPV{l}")
        vRC = self.sem(f"vRC{l}")
        tBC = self.sem(f"tBC{l}")
        vEV = self.sem(f"vEV{l}")
        lyr = f"l{l}"
        NU = 16 * 36

        @b.gpsimd
        def _(g):
            self.sW.dma(g.dma_start(out=wo_sb[:], in_=self.w_d[lyr + "_wo"].rearrange("(c p) o -> p c o", p=128)))

        @b.gpsimd
        def _(g):
            for hp in range(NHP):
                for gi in range(2):
                    idx = hp * 2 + gi
                    for p in range(9):
                        P = idx * 9 + p
                        slot = P % 2
                        if 4 * (P - 1) > 0:
                            g.wait_ge(uPV.h, 4 * (P - 1))
                        for role in range(2):
                            kg = self.attn_passes(role, gi)[p][2]
                            with (g.If_eq(self.role_g, 0) if role == 0 else g.Else()):
                                g.dma_start(out=kstg[:, slot, :],
                                            in_=self.shk_d[l][kg, hp, :, :]).then_inc(stS[slot].h, 16)
                                g.dma_start(out=vstg[:, slot, :, :],
                                            in_=self.shv_d[l][kg, hp, :, :, :].transpose([1, 0, 2])).then_inc(stS[slot].h, 16)
                        stS[slot].n += 32

        qk_ready = [(s, s.n) for s in (self.aQ + self.aK)]

        def unit_descs(role):
            descs = []
            for hp in range(NHP):
                for gi in range(2):
                    idx = hp * 2 + gi
                    passes = self.attn_passes(role, gi)
                    for p in range(9):
                        typ, qloc, kg, sel = passes[p]
                        for u, (head, ktp) in enumerate(self.UNIT_ORDER):
                            descs.append(dict(U=(idx * 9 + p) * 4 + u, idx=idx, hp=hp,
                                              gi=gi, p=p, P=idx * 9 + p, typ=typ,
                                              qloc=qloc, sel=sel, head=head, ktp=ktp))
            return descs

        def emit_qk(t, d):
            slot2 = d["U"] % 2
            if d["U"] - 1 > 0:
                t.wait_ge(uEX.h, d["U"] - 1)
            if d["U"] % 4 == 0:
                t.wait_ge(stS[d["P"] % 2].h, 32 * (d["P"] // 2 + 1))
                if d["p"] == 0 and d["idx"] > 0:
                    t.wait_ge(vEV.h, d["idx"])
            for ktl in range(2):
                kt = d["ktp"] * 2 + ktl
                h = d["head"]
                ins = t.matmul(st[:, slot2, ktl * 512:(ktl + 1) * 512],
                               kstg[h * 64:(h + 1) * 64, d["P"] % 2, kt * 128:(kt + 1) * 128],
                               self.qo_sb[h * 64:(h + 1) * 64, d["hp"],
                                          d["qloc"] * 512:(d["qloc"] + 1) * 512],
                               start=True, stop=True, tile_position=(h * 64, 0))
            ins.then_inc(uQK.h, 1)

        def emit_pv(t, d):
            slot2 = d["U"] % 2
            if d["typ"] == "D":
                t.wait_ge(uMK.h, self.mk_count(d["U"] + 1))
            else:
                t.wait_ge(uEX.h, d["U"] + 1)
            first = (d["typ"] == "D" and d["ktp"] == 0)
            h = d["head"]
            for ktl in range(2):
                kt = d["ktp"] * 2 + ktl
                qoff = kt * 128 if d["typ"] == "D" else 0
                ins = t.matmul(acc[h][:, d["sel"] * 512 + qoff:(d["sel"] + 1) * 512],
                               vstg[:, d["P"] % 2, kt, h * 65:(h + 1) * 65],
                               pt_sb[:, slot2, ktl * 512 + qoff:(ktl + 1) * 512],
                               start=(first and ktl == 0), stop=False, skip_group_check=True)
            ins.then_inc(uPV.h, 1)

        def emit_evac_mms(t, idx):
            t.wait_ge(vRC.h, idx + 1)
            t.wait_ge(uEX.h, (idx + 1) * 36)
            for head in range(2):
                for sel in range(2):
                    ins = t.matmul(st[:, head, sel * 512:(sel + 1) * 512], self.onesrow_f[:, :],
                                   rc_sb[0:1, head * 1024 + sel * 512:head * 1024 + (sel + 1) * 512],
                                   start=True, stop=True)
            ins.then_inc(tBC.h, 1)

        def emit_units_tensor(t, role):
            descs = unit_descs(role)
            pend = None
            for d in descs:
                if pend is not None and d["idx"] != pend["idx"]:
                    emit_pv(t, pend)
                    emit_evac_mms(t, pend["idx"])
                    pend = None
                emit_qk(t, d)
                if pend is not None:
                    emit_pv(t, pend)
                pend = d
            emit_pv(t, pend)
            emit_evac_mms(t, pend["idx"])

        @b.tensor
        def _(t):
            t.wait_ge(self.sG.h, self.init_gate)
            t.wait_ge(sIn.h, 1)
            for s2, v2 in qk_ready:
                t.wait_ge(s2.h, v2)
            with t.If_eq(self.role_t, 0):
                emit_units_tensor(t, 0)
            with t.Else():
                emit_units_tensor(t, 1)
        uQK.n = NU
        uPV.n = NU
        tBC.n = 16

        @b.scalar
        def _(s):
            s.wait_ge(self.sV.h, self.b_ready)
            for U in range(NU):
                slot2 = U % 2
                s.wait_ge(uQK.h, U + 1)
                if U - 1 > 0:
                    s.wait_ge(uPV.h, U - 1)
                s.activation(out=pt_sb[:, slot2, :], in_=st[:, slot2, :], func=AF.Exp,
                             bias=self.negb_sb[:, :], scale=0.125).then_inc(uEX.h, 1)
        uEX.n = NU

        @b.vector
        def _(v):
            v.wait_ge(self.sG.h, self.init_gate)
            v.memset(st.ap()[:, :, :], 0.0).then_inc(sIn.h, 1)
            for hp in range(NHP):
                for gi in range(2):
                    idx = hp * 2 + gi
                    la, lb = GROUPS[gi]
                    for p in range(2):
                        for u, (head, ktp) in enumerate(self.UNIT_ORDER):
                            U = (idx * 9 + p) * 4 + u
                            slot2 = U % 2
                            v.wait_ge(uEX.h, U + 1)
                            for ktl in range(2):
                                kt = ktp * 2 + ktl
                                off = ktl * 512 + kt * 128
                                ins = v.tensor_mul(pt_sb[:, slot2, off:off + 128],
                                                   pt_sb[:, slot2, off:off + 128], self.tri_sb[:, :])
                            ins.then_inc(uMK.h, 1)
                    v.wait_ge(uPV.h, (idx + 1) * 36)
                    for head in range(2):
                        ins = v.reciprocal(rc_sb[0:1, head * 1024:(head + 1) * 1024],
                                           acc[head][64:65, 0:1024])
                    ins.then_inc(vRC.h, 1)
                    v.wait_ge(tBC.h, idx + 1)
                    v.tensor_copy(bcast_sb[:, :], st.ap()[0:64, :, :].rearrange("p a b -> p (a b)"))
                    for head in range(2):
                        for sel, lc in ((0, la), (1, lb)):
                            ins = v.tensor_mul(self.qo_sb[head * 64:(head + 1) * 64, hp,
                                                          lc * 512:(lc + 1) * 512],
                                               acc[head][0:64, sel * 512:(sel + 1) * 512],
                                               bcast_sb[:, head * 1024 + sel * 512:head * 1024 + (sel + 1) * 512])
                    ins.then_inc(vEV.h, 1)
        uMK.n = 16 * 8
        vRC.n = 16
        vEV.n = 16
        self.attn_sems = (uPV, vEV)
        self.vEV = vEV
        self._close(psctx)
        self._close(pctx)
        self.tap(f"ao{l}", lambda: self.qo_sb.ap(), [128, NHP, TLOC], BF16)

    @staticmethod
    def mk_count(U):
        full, rem = divmod(U, 36)
        return full * 8 + min(rem, 8)

    def emit_wo(self, l):
        b = self.block
        pctx = self._ctx()
        wop = self.ps("wop", [128, 2, 512], ctx=pctx)
        sWoT = self.sem(f"sWoT{l}")
        sWoV = self.sem(f"sWoV{l}")
        self.sWoT = sWoT
        uPV, vEV = self.attn_sems
        wo_sb = self.wo_sb
        wo_ready = self.sW.n

        @b.tensor
        def _(t):
            t.wait_ge(vEV.h, 16)
            t.wait_ge(self.sW.h, wo_ready)
            for k in range(32):
                dc, tt = k // 4, k % 4
                slot = k % 2
                if k >= 2:
                    t.wait_ge(sWoV.h, k - 1)
                for hp in range(NHP):
                    ins = t.matmul(wop[:, slot, :], wo_sb[:, hp, dc * 128:(dc + 1) * 128],
                                   self.qo_sb[:, hp, tt * 512:(tt + 1) * 512],
                                   start=(hp == 0), stop=(hp == 7))
                ins.then_inc(sWoT.h, 1)
        sWoT.n = 32

        @b.vector
        def _(v):
            for k in range(32):
                dc, tt = k // 4, k % 4
                slot = k % 2
                v.wait_ge(sWoT.h, k + 1)
                bias_ap = self.prm["l0_bo"][:, dc:dc + 1] if l == 0 else 0.0
                ins = v.scalar_tensor_tensor(out=self.x_sb[:, dc, tt * 512:(tt + 1) * 512],
                                             in0=wop[:, slot, :], scalar=bias_ap,
                                             in1=self.x_sb[:, dc, tt * 512:(tt + 1) * 512],
                                             op0=ALU.add, op1=ALU.add)
                ins.then_inc(sWoV.h, 1)
        sWoV.n = 32
        self._close(pctx)
        self._close(self.pctx_wo)
        self._close(self.pctx_qkv)
        self.tap(f"xw{l}", lambda: self.x_sb.ap(), [128, 8, TLOC], F32)

    def emit_ffn(self):
        b = self.block
        pctx = self._ctx()
        wf_g = self.sb("wf_g", [128, 3, 8, 512], BF16, ctx=pctx)
        wf_u = self.sb("wf_u", [128, 3, 8, 512], BF16, ctx=pctx)
        wf_d = self.sb("wf_d", [128, 3, 4, 1024], BF16, ctx=pctx)
        silu_sb = self.sb("silu", [128, 1024], F32, ctx=pctx)
        act_sb = self.sb("act", [128, 3, 4, 256], BF16, ctx=pctx)
        pg = self.ps("pg", [128, 4, 256], ctx=pctx)
        pu = self.ps("pu", [128, 4, 256], ctx=pctx)
        pd = self.ps("pd", [128, 8, 256], ctx=pctx)
        sFG = self.sem("sFG")
        sFA = self.sem("sFA")
        sFA2 = self.sem("sFA2")
        sFD = self.sem("sFD")
        sFE = self.sem("sFE")
        sFW = self.sem("sFW")
        uPV, vEV = self.attn_sems
        gate_v = self.sV.n

        lnb_done = self.sV.n

        @b.gpsimd
        def _(g):
            g.wait_ge(uPV.h, 16 * 36)
            g.wait_ge(self.sWoT.h, 32)
            g.wait_ge(self.sV.h, lnb_done)

        def load_wave(g, w):
            for i in range(3):
                hcg = w * 3 + i
                sFW.dma(g.dma_start(out=wf_g[:, i, :, :],
                                    in_=self.w_d["l0_wg"][:, hcg * 512:(hcg + 1) * 512]
                                    .rearrange("(c p) h -> p c h", p=128)))
                sFW.dma(g.dma_start(out=wf_u[:, i, :, :],
                                    in_=self.w_d["l0_wu"][:, hcg * 512:(hcg + 1) * 512]
                                    .rearrange("(c p) h -> p c h", p=128)))
                sFW.dma(g.dma_start(out=wf_d[:, i, :, :],
                                    in_=self.w_d["l0_wd"][hcg * 512:(hcg + 1) * 512, :]
                                    .rearrange("(s p) o -> p s o", p=128)))

        wave_w_ready = []

        @b.gpsimd
        def _(g):
            for w in range(2):
                if w == 1:
                    g.wait_ge(sFD.h, 24)
                load_wave(g, w)
                wave_w_ready.append(sFW.n)

        @b.tensor
        def _(t):
            t.wait_ge(self.sV.h, gate_v)

            def down(mp):
                wp, kp, ip = mp // 24, (mp // 3) % 8, mp % 3
                t.wait_ge(sFA2.h, mp + 1)
                if ip == 0:
                    kkp = wp * 8 + kp
                    if kkp >= 1:
                        t.wait_ge(sFE.h, kkp)
                for hs in range(4):
                    for dco in range(8):
                        ins = t.matmul(pd[:, dco, :], wf_d[:, ip, hs, dco * 128:(dco + 1) * 128],
                                       act_sb[:, ip, hs, :],
                                       start=(ip == 0 and hs == 0 and dco % 2 == 0),
                                       stop=False, skip_group_check=True)
                ins.then_inc(sFD.h, 1)

            for m in range(48):
                w = m // 24
                if m % 24 == 0:
                    if m >= 1:
                        down(m - 1)
                    t.wait_ge(sFW.h, wave_w_ready[w])
                if m >= 1:
                    t.wait_ge(sFA2.h, m)
                i = m % 3
                k = (m // 3) % 8
                ts = slice(k * 256, (k + 1) * 256)
                for hs in range(4):
                    for dc in range(8):
                        t.matmul(pg[:, hs, :], wf_g[:, i, dc, hs * 128:(hs + 1) * 128],
                                 self.xn_sb[:, dc, ts], start=(dc == 0), stop=(dc == 7))
                for hs in range(4):
                    for dc in range(8):
                        ins = t.matmul(pu[:, hs, :], wf_u[:, i, dc, hs * 128:(hs + 1) * 128],
                                       self.xn_sb[:, dc, ts], start=(dc == 0), stop=(dc == 7))
                ins.then_inc(sFG.h, 1)
                if m >= 1 and m % 24 != 0:
                    down(m - 1)
            down(47)
        sFG.n = 48
        sFD.n = 48

        @b.scalar
        def _(s):
            for m in range(48):
                s.wait_ge(sFG.h, m + 1)
                s.activation(out=silu_sb[:, :], in_=pg.ap()[:, :, :].rearrange("p a b -> p (a b)"),
                             func=AF.Silu, scale=1.0 / 127.0).then_inc(sFA.h, 1)
        sFA.n = 48

        @b.vector
        def _(v):
            for m in range(48):
                i = m % 3
                v.wait_ge(sFA.h, m + 1)
                if m >= 3:
                    v.wait_ge(sFD.h, m - 2)
                v.scalar_tensor_tensor(out=act_sb[:, i, :, :],
                                       in0=pu.ap()[:, :, :], scalar=1.0 / 127.0,
                                       in1=silu_sb.ap().rearrange("p (a b) -> p a b", a=4),
                                       op0=ALU.mult, op1=ALU.mult).then_inc(sFA2.h, 1)
                if i == 2:
                    kk = m // 3
                    k = kk % 8
                    v.wait_ge(sFD.h, (kk + 1) * 3)
                    ts = slice(k * 256, (k + 1) * 256)
                    for dco in range(8):
                        ins = v.scalar_tensor_tensor(out=self.x_sb[:, dco, ts], in0=pd[:, dco, :],
                                                     scalar=1.0 / 127.0, in1=self.x_sb[:, dco, ts],
                                                     op0=ALU.mult, op1=ALU.add)
                    ins.then_inc(sFE.h, 1)
        sFA2.n = 48
        sFE.n = 16
        self._close(pctx)
        self.xdep = (sFE, 16)
        self.tap("xf", lambda: self.x_sb.ap(), [128, 8, TLOC], F32)

    def emit_router(self):
        b = self.block
        pctx = self._ctx()
        self.moe_ctx = pctx
        self.gate_fm = self.sb("gate_fm", [16, TLOC], F32, ctx=pctx)
        rctx = self._ctx()
        rw_sb = self.sb("rw", [128, 8, 16], BF16, ctx=rctx)
        pe_sb = self.sb("pe", [128, 16], F32, ctx=rctx)
        sume_sb = self.sb("sume", [128, 1], F32, ctx=rctx)
        rsume_sb = self.sb("rsume", [128, 1], F32, ctx=rctx)
        rjunk_sb = self.sb("rjunk", [128, 1], F32, ctx=rctx)
        m8_sb = self.sb("m8", [128, 8], F32, ctx=rctx)
        prb_sb = self.sb("prb", [128, 2, 16], F32, ctx=rctx)
        prp = self.ps("prp", [128, 2, 512], ctx=rctx)
        gtp = self.ps("gtp", [16, 2, 512], ctx=rctx)
        sRT = self.sem("sRT")
        sRS = self.sem("sRS")
        sRMu = self.sem("sRMu")
        sRRc = self.sem("sRRc")
        sRGt = self.sem("sRGt")
        sRTr = self.sem("sRTr")
        sRCp = self.sem("sRCp")
        xn_ready = self.sV.n

        @b.gpsimd
        def _(g):
            g.wait_ge(self.sV.h, xn_ready)
            self.sW.dma(g.dma_start(out=rw_sb[:], in_=self.w_d["l1_rw"].rearrange("(c p) e -> p c e", p=128)))
        rw_ready = self.sW.n

        @b.tensor
        def _(t):
            t.wait_ge(self.sW.h, rw_ready)
            t.wait_ge(self.sV.h, xn_ready)
            t.wait_ge(self.sG.h, self.init_gate)
            for k in range(16):
                slot = k % 2
                if k >= 2:
                    t.wait_ge(sRS.h, k - 1)
                for dc in range(8):
                    ins = t.matmul(prp[:, slot, 0:16], self.xn_sb[:, dc, k * 128:(k + 1) * 128],
                                   rw_sb[:, dc, :], start=(dc == 0), stop=(dc == 7))
                ins.then_inc(sRT.h, 1)
                if k >= 1:
                    t.wait_ge(sRGt.h, k)
                    if k >= 3:
                        t.wait_ge(sRCp.h, k - 2)
                    t.transpose(gtp[:, (k - 1) % 2, 0:128], prb_sb[:, (k - 1) % 2, :],
                                self.ident_f[:, :]).then_inc(sRTr.h, 1)
            t.wait_ge(sRGt.h, 16)
            t.wait_ge(sRCp.h, 14)
            t.transpose(gtp[:, 15 % 2, 0:128], prb_sb[:, 15 % 2, :],
                        self.ident_f[:, :]).then_inc(sRTr.h, 1)
        sRT.n = 16
        sRTr.n = 16

        @b.scalar
        def _(s):
            for k in range(16):
                slot = k % 2
                s.wait_ge(sRT.h, k + 1)
                if k >= 1:
                    s.wait_ge(sRMu.h, k)
                    s.wait_ge(sRRc.h, k)
                s.activation(out=pe_sb[:, :], in_=prp[:, slot, 0:16], func=AF.Exp,
                             bias=0.0, scale=1.0, accum_out=sume_sb[:, :]).then_inc(sRS.h, 1)
        sRS.n = 16

        @b.vector
        def _(v):
            for k in range(16):
                slot = k % 2
                v.wait_ge(sRS.h, k + 1)
                v.reciprocal(rsume_sb[:, :], sume_sb[:, :]).then_inc(sRRc.h, 1)
                if k >= 1:
                    v.scalar_tensor_tensor(out=prb_sb[:, 1 - slot, :], in0=prb_sb[:, 1 - slot, :],
                                           scalar=m8_sb[:, 3:4], in1=prb_sb[:, 1 - slot, :],
                                           op0=ALU.is_ge, op1=ALU.mult).then_inc(sRGt.h, 1)
                else:
                    v.tensor_copy(rjunk_sb[:, :], sume_sb[:, :])
                v.tensor_scalar_mul(prb_sb[:, slot, :], pe_sb[:, :], rsume_sb[:, 0:1]).then_inc(sRMu.h, 1)
                if k >= 2:
                    v.wait_ge(sRTr.h, k - 1)
                    v.tensor_copy(self.gate_fm[:, (k - 2) * 128:(k - 1) * 128],
                                  gtp[:, k % 2, 0:128]).then_inc(sRCp.h, 1)
                else:
                    v.tensor_copy(rjunk_sb[:, :], rsume_sb[:, :])
                v.max(out=m8_sb[:, :], in_=prb_sb[:, slot, :])
            v.tensor_copy(rjunk_sb[:, :], rsume_sb[:, :])
            v.scalar_tensor_tensor(out=prb_sb[:, 1, :], in0=prb_sb[:, 1, :],
                                   scalar=m8_sb[:, 3:4], in1=prb_sb[:, 1, :],
                                   op0=ALU.is_ge, op1=ALU.mult).then_inc(sRGt.h, 1)
            v.wait_ge(sRTr.h, 15)
            v.tensor_copy(self.gate_fm[:, 14 * 128:15 * 128], gtp[:, 0, 0:128]).then_inc(sRCp.h, 1)
            v.wait_ge(sRTr.h, 16)
            v.tensor_copy(self.gate_fm[:, 15 * 128:16 * 128], gtp[:, 1, 0:128]).then_inc(sRCp.h, 1)
        sRMu.n = 16
        sRRc.n = 16
        sRGt.n = 16
        sRCp.n = 16
        self.router_done = sRCp
        self._close(rctx)
        self.tap("gate", lambda: self.gate_fm.ap(), [16, TLOC], F32)

    def emit_moe(self):
        b = self.block
        pctx = self.moe_ctx
        WAVES = [4, 4, 4, 3, 3][:self.moe_waves]
        EXPERTS = ([("s", 0), ("s", 1)] + [("r", i) for i in range(16)])[:sum(WAVES)]
        NW = max(WAVES)
        wup = self.sb("wup", [128, NW, 8, 512], BF16, ctx=pctx)
        wdn = self.sb("wdn", [128, NW, 4, 1024], BF16, ctx=pctx)
        gstage = self.sb("gstage", [1, 2, 1024], F32, ctx=pctx)
        gelu_sb = self.sb("gelu", [128, 1024], F32, ctx=pctx)
        act_sb = self.sb("mact", [128, 2, 4, 256], BF16, ctx=pctx)
        pu = self.ps("mpu", [128, 4, 256], ctx=pctx)
        gbp = self.ps("gbp", [128, 256], ctx=pctx)
        pd = self.ps("mpd", [128, 8, 256], ctx=pctx)
        sMW = self.sem("sMW")
        sMU = self.sem("sMU")
        sMG = self.sem("sMG")
        sMA = self.sem("sMA")
        sMB = self.sem("sMB")
        sMD = self.sem("sMD")
        sME = self.sem("sME")
        sGS = [self.sem(f"sGS{i}") for i in range(2)]
        gate_sem = self.router_done

        sched_m_of = {}
        cumd = []
        tot = 0
        for w_, wn_ in enumerate(WAVES):
            for k_ in range(8):
                tot += wn_
                cumd.append(tot)

        def moe_down(t, m, entry):
            w, k, i, (typ, ei) = entry
            g = w * 8 + k
            t.wait_ge(sMA.h, m + 1)
            if i == 0 and g >= 1:
                t.wait_ge(sME.h, g)
            for es in range(4):
                for dco in range(8):
                    ins = t.matmul(pd[:, dco, :], wdn[:, i, es, dco * 128:(dco + 1) * 128],
                                   act_sb[:, m % 2, es, :],
                                   start=(i == 0 and es == 0 and dco % 2 == 0),
                                   stop=False, skip_group_check=True)
            ins.then_inc(sMD.h, 1)

        sched = []
        e0 = 0
        for w, wn in enumerate(WAVES):
            for k in range(8):
                for i in range(wn):
                    sched.append((w, k, i, EXPERTS[e0 + i]))
            e0 += wn
        M = len(sched)
        wn_of = [WAVES[e[0]] for e in sched]

        cum_m_tt = cumd

        wave_r = []
        e0 = 0
        for wn in WAVES:
            rts = [EXPERTS[e0 + i][1] for i in range(wn) if EXPERTS[e0 + i][0] == "r"]
            wave_r.append((min(rts) if rts else 0, len(rts)))
            e0 += wn

        @b.gpsimd
        def _(g):
            g.wait_ge(gate_sem.h, 16)
            e0 = 0
            loads = []
            for w, wn in enumerate(WAVES):
                if w >= 1:
                    last_m = sum(WAVES[:w]) * 8
                    g.wait_ge(sMD.h, last_m)
                for i in range(wn):
                    typ, ei = EXPERTS[e0 + i]
                    upd = self.w_d["l1_su"][ei] if typ == "s" else self.w_d["l1_ru"][ei]
                    dnd = self.w_d["l1_sd"][ei] if typ == "s" else self.w_d["l1_rd"][ei]
                    sMW.dma(g.dma_start(out=wup[:, i, :, :],
                                        in_=upd.rearrange("(c p) e -> p c e", p=128)))
                    sMW.dma(g.dma_start(out=wdn[:, i, :, :],
                                        in_=dnd.rearrange("(s p) o -> p s o", p=128)))
                e0 += wn
                loads.append(sMW.n)
                r0, nr = wave_r[w]
                for k in range(8):
                    gg = w * 8 + k
                    slot = gg % 2
                    if gg >= 2:
                        g.wait_ge(sMB.h, cum_m_tt[gg - 2])
                    if nr:
                        g.dma_start(out=gstage[0:1, slot, 0:nr * 256],
                                    in_=self.gate_fm[r0:r0 + nr, k * 256:(k + 1) * 256]).then_inc(sGS[slot].h, 16)
                    else:
                        g.dma_start(out=gstage[0:1, slot, 0:256],
                                    in_=self.gate_fm[0:1, k * 256:(k + 1) * 256]).then_inc(sGS[slot].h, 16)
                    sGS[slot].n += 16
            self.moe_loads = loads

        @b.tensor
        def _(t):
            t.wait_ge(gate_sem.h, 16)
            t.wait_ge(self.sG.h, self.init_gate)
            for m, (w, k, i, (typ, ei)) in enumerate(sched):
                if i == 0 and k == 0 and m >= 1:
                    moe_down(t, m - 1, sched[m - 1])
                if i == 0:
                    t.wait_ge(sMW.h, self.moe_loads[w])
                ts = slice(k * 256, (k + 1) * 256)
                if m >= 1:
                    t.wait_ge(sMG.h, m)
                for es in range(4):
                    for dc in range(8):
                        ins = t.matmul(pu[:, es, :], wup[:, i, dc, es * 128:(es + 1) * 128],
                                       self.xn_sb[:, dc, ts], start=(dc == 0), stop=(dc == 7))
                ins.then_inc(sMU.h, 1)
                if typ == "r":
                    gg = w * 8 + k
                    if m >= 1:
                        t.wait_ge(sMA.h, m)
                    t.wait_ge(sGS[gg % 2].h, 16 * (gg // 2 + 1))
                    r0 = wave_r[w][0]
                    t.matmul(gbp[:, :], self.onesrow_f[:, :],
                             gstage[0:1, gg % 2, (ei - r0) * 256:(ei - r0 + 1) * 256],
                             start=True, stop=True).then_inc(sMB.h, 1)
                else:
                    t.nop().then_inc(sMB.h, 1)
                if m >= 1 and not (i == 0 and k == 0):
                    moe_down(t, m - 1, sched[m - 1])
            moe_down(t, M - 1, sched[M - 1])
        sMU.n = M
        sMB.n = M
        sMD.n = M

        @b.scalar
        def _(s):
            for m, (w, k, i, (typ, ei)) in enumerate(sched):
                s.wait_ge(sMU.h, m + 1)
                if m >= 1:
                    s.wait_ge(sMA.h, m)
                if typ == "s":
                    if m >= 2:
                        s.wait_ge(sMD.h, m - 1)
                    s.activation(out=act_sb[:, m % 2, :, :], in_=pu.ap()[:, :, :],
                                 func=AF.Gelu).then_inc(sMG.h, 1)
                else:
                    s.activation(out=gelu_sb.ap().rearrange("p (a b) -> p a b", a=4),
                                 in_=pu.ap()[:, :, :], func=AF.Gelu).then_inc(sMG.h, 1)
        sMG.n = M

        @b.vector
        def _(v):
            for m, (w, k, i, (typ, ei)) in enumerate(sched):
                v.wait_ge(sMG.h, m + 1)
                if typ == "r":
                    v.wait_ge(sMB.h, m + 1)
                    if m >= 2:
                        v.wait_ge(sMD.h, m - 1)
                    v.tensor_mul(act_sb[:, m % 2, :, :],
                                 gelu_sb.ap().rearrange("p (a b) -> p a b", a=4),
                                 gbp.ap()[:, :].rearrange("p (a b) -> p a b", a=1).broadcast_to([128, 4, 256])).then_inc(sMA.h, 1)
                else:
                    v.nop().then_inc(sMA.h, 1)
                if i == wn_of[m] - 1:
                    g = w * 8 + k
                    v.wait_ge(sMD.h, cumd[g])
                    ts = slice(k * 256, (k + 1) * 256)
                    for dco in range(8):
                        ins = v.scalar_tensor_tensor(out=self.x_sb[:, dco, ts], in0=pd[:, dco, :],
                                                     scalar=1.0, in1=self.x_sb[:, dco, ts],
                                                     op0=ALU.mult, op1=ALU.add)
                    ins.then_inc(sME.h, 1)
        sMA.n = M
        sME.n = 8 * len(WAVES)
        self.x_final_gate = (sME, 8 * len(WAVES))
        self._moe_cumd = cumd
        self._close(pctx)
        self.tap("xm", lambda: self.x_sb.ap(), [128, 8, TLOC], F32)



    def emit_out(self):
        b = self.block
        gsem, gval = self.x_final_gate

        @b.sync
        def _(s):
            s.wait_ge(gsem.h, gval)
            self.sY.dma(s.dma_start(out=self.y_d.rearrange("(c p) t -> p c t", p=128),
                                    in_=self.x_sb[:]))
            s.wait_ge(self.sY.h, self.sY.n)

        @b.gpsimd
        def _(g):
            g.wait_ge(self.sY.h, self.sY.n)

    def emit_taps(self):
        if not self.tap_outs:
            return
        b = self.block

        @b.gpsimd
        def _(g):
            for s in self.all_sems:
                if s.n > 0:
                    g.wait_ge(s.h, s.n)
            for name, (dram, ap_fn) in self.tap_outs.items():
                self.sY.dma(g.dma_start(out=dram, in_=ap_fn()))
            g.wait_ge(self.sY.h, self.sY.n)


_CACHE = {}


def get_nc(taps=(), stop_after="out", moe_waves=5, router_stage=4):
    key = (tuple(sorted(taps)), stop_after, moe_waves, router_stage)
    if key not in _CACHE:
        _CACHE[key] = MK(taps=taps, stop_after=stop_after, moe_waves=moe_waves,
                         router_stage=router_stage).build()
    return _CACHE[key]


def run(inputs, taps=(), stop_after="out", raw=False, moe_waves=5, router_stage=4, **kw):
    nonce = int(np.random.randint(1, 2 ** 30))
    in_maps = host_prepare(inputs, nonce)
    nc = get_nc(taps, stop_after, moe_waves, router_stage)
    res = run_bass_kernel_spmd(nc, in_maps, core_ids=list(range(8)), **kw)
    if raw:
        return res
    return host_assemble(res.results)


def kernel(**inputs):
    nonce = int(np.random.randint(1, 2 ** 30))
    in_maps = host_prepare(inputs, nonce)
    nc = get_nc()
    res = run_bass_kernel_spmd(nc, in_maps, core_ids=list(range(8)))
    return host_assemble(res.results)


# version 03
# speedup vs baseline: 1.0014x; 1.0014x over previous
import contextlib
import numpy as np
import ml_dtypes
import concourse.bass as bass
import concourse.mybir as mybir
import concourse.expressions as ex
from bass_rust import RuntimeValue
from concourse.bass_utils import run_bass_kernel_spmd

F32 = mybir.dt.float32
BF16 = mybir.dt.bfloat16
I32 = mybir.dt.int32
AF = mybir.ActivationFunctionType
ALU = mybir.AluOpType

D = 1024
SEQ = 4096
NB = 4
CH = 512
NCH = 8
TLOC = 2048
NHP = 8
DH = 64
HID = 3072
EPS = 1e-5
ROLE_CHUNKS = [[0, 3, 4, 7], [1, 2, 5, 6]]
GROUPS = [(0, 3), (1, 2)]

NEG_SLOPE = None


def _rv(x):
    return RuntimeValue(x)


def _binop(a, b, op):
    if isinstance(b, int):
        b = ex.make_scalar_value(b)
    return RuntimeValue(ex.binary_op(a, b, op))


class Sem:
    def __init__(self, ctx, nc, name):
        self.h = ctx.enter_context(nc.semaphore(name))
        self.n = 0

    def inc(self, instr, k=1):
        instr.then_inc(self.h, k)
        self.n += k
        return self.n

    def dma(self, instr):
        return self.inc(instr, 16)

    def wait(self, eng, val=None):
        eng.wait_ge(self.h, self.n if val is None else val)



def rope_tables():
    inv = 10000.0 ** (-np.arange(32) / 32.0)
    ang = np.arange(SEQ)[:, None] * inv[None, :]
    c = np.cos(ang).T.astype(np.float32)
    s = np.sin(ang).T.astype(np.float32)
    C64 = np.concatenate([c, c], axis=0)
    S64 = np.concatenate([s, -s], axis=0)
    C = np.concatenate([C64, C64], axis=0)
    Sm = np.concatenate([S64, S64], axis=0)
    return np.ascontiguousarray(C), np.ascontiguousarray(Sm)


def perm_l0_qk_cols(W):
    W = W.reshape(D, 16, 64)
    perm = np.concatenate([np.arange(0, 64, 2), np.arange(1, 64, 2)])
    W = W[:, :, perm]
    return np.ascontiguousarray(W.reshape(D, D))


def perm_l0_qk_bias(b):
    b = b.reshape(16, 64)
    perm = np.concatenate([np.arange(0, 64, 2), np.arange(1, 64, 2)])
    return np.ascontiguousarray(b[:, perm].reshape(D))


def chunk_feat(v):
    return np.ascontiguousarray(v.reshape(8, 128).T)


def host_prepare(inputs, nonce):
    f = lambda k: np.asarray(inputs[k], dtype=np.float32)
    i = lambda k: np.asarray(inputs[k], dtype=np.int32)
    x = f("L0_attn_x")

    C, Sm = rope_tables()
    tri = np.tril(np.ones((128, 128), dtype=np.float32)).T
    tri = np.ascontiguousarray(tri)

    shared = {
        "l0_wq": perm_l0_qk_cols(f("L0_attn_Wq")),
        "l0_wk": perm_l0_qk_cols(f("L0_attn_Wk")),
        "l0_wv": np.ascontiguousarray(f("L0_attn_Wv")),
        "l0_wo": np.ascontiguousarray(f("L0_attn_Wo")),
        "l0_bq": chunk_feat(perm_l0_qk_bias(f("L0_attn_bq"))),
        "l0_bk": chunk_feat(perm_l0_qk_bias(f("L0_attn_bk"))),
        "l0_bv": np.ascontiguousarray(f("L0_attn_bv").reshape(1, D)),
        "l0_bo": chunk_feat(f("L0_attn_bo")),
        "l0_gamma": chunk_feat(f("L0_attn_ln_gamma")),
        "l0_beta": chunk_feat(f("L0_attn_ln_beta")),
        "l0_wg": np.ascontiguousarray(i("L0_ffn_w_gate").T),
        "l0_wu": np.ascontiguousarray(i("L0_ffn_w_up").T),
        "l0_wd": np.ascontiguousarray(i("L0_ffn_w_down").T),
        "l1_wq": np.ascontiguousarray(f("L1_attn_Wq").T),
        "l1_wk": np.ascontiguousarray(f("L1_attn_Wk").T),
        "l1_wv": np.ascontiguousarray(f("L1_attn_Wv").T),
        "l1_wo": np.ascontiguousarray(f("L1_attn_Wo").T),
        "l1_su": np.ascontiguousarray(f("L1_ffn_shared_up").transpose(0, 2, 1)),
        "l1_sd": np.ascontiguousarray(f("L1_ffn_shared_down").transpose(0, 2, 1)),
        "l1_ru": np.ascontiguousarray(f("L1_ffn_routed_up").transpose(0, 2, 1)),
        "l1_rd": np.ascontiguousarray(f("L1_ffn_routed_down").transpose(0, 2, 1)),
        "l1_rw": np.ascontiguousarray(f("L1_ffn_router_w").T),
        "trimask": tri,
    }

    in_maps = []
    for c in range(8):
        seq, role = c // 2, c % 2
        chunks = ROLE_CHUNKS[role]
        xs = np.concatenate([x[seq, g * CH:(g + 1) * CH, :] for g in chunks], axis=0)
        pos = np.concatenate([np.arange(g * CH, (g + 1) * CH) for g in chunks])
        m = dict(shared)
        m["x"] = np.ascontiguousarray(xs.T)
        m["cl"] = np.ascontiguousarray(C[:, pos])
        m["sl"] = np.ascontiguousarray(Sm[:, pos])
        m["nonce"] = np.array([[nonce]], dtype=np.int32)
        in_maps.append(m)
    return in_maps


def host_assemble(results):
    y = np.empty((NB, SEQ, D), dtype=np.float32)
    for c in range(8):
        seq, role = c // 2, c % 2
        yc = results[c]["y"].T
        for li, g in enumerate(ROLE_CHUNKS[role]):
            y[seq, g * CH:(g + 1) * CH, :] = yc[li * CH:(li + 1) * CH, :]
    return y



class MK:

    def __init__(self, taps=(), stop_after="out", moe_waves=5, router_stage=4):
        self.taps = set(taps)
        self.stop_after = stop_after
        self.moe_waves = moe_waves
        self.router_stage = router_stage
        self.nc = bass.Bass("TRN2", target_bir_lowering=False, detect_race_conditions=False)
        self.ctx = contextlib.ExitStack()
        self.tap_outs = {}

    def _ctx(self):
        c = contextlib.ExitStack()
        if not hasattr(self, "live"):
            self.live = []
        self.live.append(c)
        return c

    def _close(self, c):
        assert self.live and self.live[-1] is c, "phase ctx close out of order"
        self.live.pop()
        c.close()

    def _close_all(self):
        for c in reversed(getattr(self, "live", [])):
            c.close()
        self.live = []

    def sem(self, name):
        s = Sem(self.ctx, self.nc, name)
        if not hasattr(self, "all_sems"):
            self.all_sems = []
        self.all_sems.append(s)
        return s

    def sb(self, name, shape, dt, ctx=None):
        self._uid = getattr(self, "_uid", 0) + 1
        return (ctx or self.ctx).enter_context(
            self.nc.sbuf_tensor(f"{name}_{self._uid}", shape, dt))

    def ps(self, name, shape, dt=F32, ctx=None):
        self._uid = getattr(self, "_uid", 0) + 1
        return (ctx or self.ctx).enter_context(
            self.nc.psum_tensor(f"{name}_{self._uid}", shape, dt))

    def ln_scratch(self, ctx):
        self.tmp_sb = self.sb("tmp_sb", [128, TLOC], F32, ctx=ctx)
        self.tmp2_sb = self.sb("tmp2_sb", [128, TLOC], F32, ctx=ctx)
        self.s1_sb = self.sb("s1_sb", [1, TLOC], F32, ctx=ctx).ap()
        self.s2_sb = self.sb("s2_sb", [1, TLOC], F32, ctx=ctx).ap()
        self.sc_sb = self.sb("sc_sb", [1, TLOC], F32, ctx=ctx).ap()
        self.sc2_sb = self.sb("sc2_sb", [1, TLOC], F32, ctx=ctx).ap()
        self.mu_sb = self.sb("mu_sb", [1, TLOC], F32, ctx=ctx).ap()
        self.a1_sb = self.sb("a1_sb", [1, TLOC], F32, ctx=ctx).ap()
        self.aa_sb = self.sb("aa_sb", [1, TLOC], F32, ctx=ctx).ap()

    def xbf_ap(self):
        return self.tmp_sb.ap().bitcast(BF16)[:, 0:TLOC]

    def sqbf_ap(self):
        return self.tmp2_sb.ap().bitcast(BF16)[:, 0:TLOC]

    def tap(self, name, ap_fn, shape, dt=F32):
        if name in self.taps:
            d = self.nc.dram_tensor("tap_" + name, shape, dt, kind="ExternalOutput").ap()
            self.tap_outs[name] = (d, ap_fn)

    def build(self):
        nc = self.nc
        self.x_d = nc.dram_tensor("x", [D, TLOC], F32, kind="ExternalInput").ap()
        self.cl_d = nc.dram_tensor("cl", [128, TLOC], F32, kind="ExternalInput").ap()
        self.sl_d = nc.dram_tensor("sl", [128, TLOC], F32, kind="ExternalInput").ap()
        self.tri_d = nc.dram_tensor("trimask", [128, 128], F32, kind="ExternalInput").ap()
        self.nonce_d = nc.dram_tensor("nonce", [1, 1], I32, kind="ExternalInput").ap()
        self.w_d = {}
        for n, shp in [
            ("l0_wq", [D, D]), ("l0_wk", [D, D]), ("l0_wv", [D, D]), ("l0_wo", [D, D]),
            ("l1_wq", [D, D]), ("l1_wk", [D, D]), ("l1_wv", [D, D]), ("l1_wo", [D, D]),
            ("l0_bq", [128, 8]), ("l0_bk", [128, 8]), ("l0_bo", [128, 8]),
            ("l0_bv", [1, D]), ("l0_gamma", [128, 8]), ("l0_beta", [128, 8]),
            ("l1_su", [2, D, 512]), ("l1_sd", [2, 512, D]),
            ("l1_ru", [16, D, 512]), ("l1_rd", [16, 512, D]), ("l1_rw", [D, 16]),
        ]:
            self.w_d[n] = nc.dram_tensor(n, shp, F32, kind="ExternalInput").ap()
        for n, shp in [("l0_wg", [D, HID]), ("l0_wu", [D, HID]), ("l0_wd", [HID, D])]:
            self.w_d[n] = nc.dram_tensor(n, shp, I32, kind="ExternalInput").ap()
        self.y_d = nc.dram_tensor("y", [D, TLOC], F32, kind="ExternalOutput").ap()
        self.shk_d = [nc.dram_tensor(f"shk{l}", [NCH, NHP, 128, CH], BF16,
                                     kind="Internal", addr_space="Shared").ap() for l in range(2)]
        self.shv_d = [nc.dram_tensor(f"shv{l}", [NCH, NHP, 4, 128, 130], BF16,
                                     kind="Internal", addr_space="Shared").ap() for l in range(2)]
        self.shflag_d = [nc.dram_tensor(f"shflag{l}", [2, 1, 4], I32,
                                        kind="Internal", addr_space="Shared").ap() for l in range(2)]
        self.shnorm_d = [nc.dram_tensor(f"shnorm{l}", [2, 1, 4], F32,
                                        kind="Internal", addr_space="Shared").ap() for l in range(2)]

        c = self.ctx
        nc_ = nc
        self.x_sb = self.sb("x_sb", [128, 8, TLOC], F32)
        self.xn_sb = self.sb("xn_sb", [128, 8, TLOC], BF16)
        self.tri_sb = self.sb("tri_sb", [128, 128], BF16)
        self.ident = self.sb("ident", [128, 128], BF16)
        self.ident_f = self.sb("ident_f", [128, 128], F32)
        self.onescol_bf = self.sb("onescol_bf", [128, 1], BF16)
        self.onesrow_f = self.sb("onesrow_f", [1, 128], F32)
        self.onesrow_bf = self.sb("onesrow_bf", [1, 128], BF16)
        self.prm = {}
        for n in ["l0_bq", "l0_bk", "l0_bo", "l0_gamma", "l0_beta"]:
            self.prm[n] = self.sb(n + "_sb", [128, 8], F32)
        self.prm["l0_bv"] = self.sb("l0_bv_sb", [1, D], BF16)
        self.nonce_sb = self.sb("nonce_sb", [1, 4], I32)
        self.flag_sb = self.sb("flag_sb", [1, 4], I32)
        self.pflag_sb = self.sb("pflag_sb", [1, 4], I32)
        self.mynorm_sb = self.sb("mynorm_sb", [1, 4], F32)
        self.pnorm_sb = self.sb("pnorm_sb", [1, 4], F32)
        self.negb_sb = self.sb("negb_sb", [128, 1], F32)
        self.bscr_sb = self.sb("bscr_sb", [1, 8], F32)
        self.normc_sb = self.sb("normc_sb", [128, 4], F32)
        for n in ["sLD", "sW", "sT", "sV", "sSc", "sG", "sPol", "sPr", "sY",
                  "sKW", "sFl", "sEv", "sRC", "sBC", "sQN"]:
            setattr(self, n, self.sem(n))
        self.aQ = [self.sem(f"aQ{i}") for i in range(2)]
        self.aK = [self.sem(f"aK{i}") for i in range(2)]
        self.kW = [self.sem(f"kW{i}") for i in range(2)]
        self.vW = [self.sem(f"vW{i}") for i in range(2)]
        self.stK = [self.sem(f"stK{i}") for i in range(2)]
        self.wF = [self.sem(f"wF{i}") for i in range(2)]

        self.block = c.enter_context(nc.Block())
        self.emit_init()
        order = ["ln_a0", "qkv0", "xchg0", "attn0", "wo0", "ln_b0", "ffn",
                 "ln_a1", "qkv1", "xchg1", "attn1", "wo1", "ln_b1", "moe", "out"]
        if self.stop_after == "router":
            order[order.index("moe")] = "moe"
            order = order[:order.index("moe") + 1]
        self.xdep = (self.sLD, self.sLD.n)
        for ph in order:
            l = int(ph[-1]) if ph[-1] in "01" else None
            kind = ph.rstrip("01")
            if kind == "ln_a":
                self.emit_ln_a(l)
            elif kind == "qkv":
                self.emit_qkv(l)
            elif kind == "xchg":
                self.emit_xchg(l)
            elif kind == "attn":
                self.emit_attn(l)
            elif kind == "wo":
                self.emit_wo(l)
            elif kind == "ln_b":
                self.emit_ln_b(l)
            elif kind == "ffn":
                self.emit_ffn()
            elif kind == "moe":
                self.emit_router()
                if self.stop_after != "router":
                    self.emit_moe()
            elif kind == "out":
                self.emit_out()
            if ph == self.stop_after:
                break
        self.emit_taps()
        self._close_all()
        self.ctx.close()
        return self.nc

    def emit_init(self):
        b = self.block

        @b.sync
        def _(s):
            self.sLD.dma(s.dma_start(out=self.x_sb[:],
                                     in_=self.x_d.rearrange("(c p) t -> p c t", p=128)))
            for n in ["l0_bq", "l0_bk", "l0_bo", "l0_gamma", "l0_beta"]:
                self.sLD.dma(s.dma_start(out=self.prm[n][:], in_=self.w_d[n][:, :]))
            self.sLD.dma(s.dma_start(out=self.nonce_sb[:, 0:1], in_=self.nonce_d[:, :]))

        @b.gpsimd
        def _(g):
            self.sW.dma(g.dma_start(out=self.tri_sb[:], in_=self.tri_d[:, :]))
            self.sW.dma(g.dma_start(out=self.prm["l0_bv"][:], in_=self.w_d["l0_bv"][:, :]))
            g.iota(self.ident[:], pattern=[[1, 128]], base=0, channel_multiplier=-1,
                   allow_small_or_imprecise_dtypes=True).then_inc(self.sG.h, 1)
            g.iota(self.ident_f[:], pattern=[[1, 128]], base=0, channel_multiplier=-1,
                   allow_small_or_imprecise_dtypes=True).then_inc(self.sG.h, 1)
            g.memset(self.onescol_bf[:], 1.0).then_inc(self.sG.h, 1)
            g.memset(self.onesrow_f[:], 1.0).then_inc(self.sG.h, 1)
            g.memset(self.onesrow_bf[:], 1.0).then_inc(self.sG.h, 1)
            g.memset(self.mynorm_sb[:], 0.0)
            g.memset(self.pnorm_sb[:], 0.0)
            g.memset(self.bscr_sb[:], 0.0)
            g.memset(self.negb_sb[:], 0.0)
            g.memset(self.flag_sb[:], 0)
            g.memset(self.pflag_sb[:], 0).then_inc(self.sG.h, 1)
            self.role_g = g.to_reg(_binop(g.partition_id(), 2, ALU.mod))
            self.sG.n += 6

        @b.tensor
        def _(t):
            self.role_t = t.to_reg(_binop(t.partition_id(), 2, ALU.mod))

        self.init_gate = self.sG.n

        @b.vector
        def _(v):
            self.sG.wait(v)
            v.tensor_scalar(out=self.ident[:], in0=self.ident[:], scalar1=0.0,
                            scalar2=None, op0=ALU.is_equal)
            ins = v.tensor_scalar(out=self.ident_f[:], in0=self.ident_f[:], scalar1=0.0,
                                  scalar2=None, op0=ALU.is_equal)
            self.sV.inc(ins)

    def _stats(self, l, xdep, want_aa):
        b = self.block
        pctx = self._ctx()
        s1p = self.ps("s1p", [1, TLOC], ctx=pctx)
        s2p = self.ps("s2p", [1, TLOC], ctx=pctx)
        vbase = self.sV.n
        scbase = self.sSc.n

        @b.vector
        def _(v):
            for c in range(8):
                if c == 0:
                    xdep[0].wait(v, xdep[1])
                self.sV.inc(v.tensor_copy(self.xbf_ap(), self.x_sb[:, c, :]))
                self.sV.inc(v.tensor_mul(self.sqbf_ap(), self.xbf_ap(), self.xbf_ap()))
                if c < 7:
                    self.sT.wait(v, self.sT.n + (c + 1) * 8)

        @b.tensor
        def _(t):
            t.wait_ge(self.sG.h, self.init_gate)
            for c in range(8):
                self.sV.wait(t, vbase + 2 * c + 1)
                for tt in range(4):
                    t.matmul(s1p[0:1, tt * 512:(tt + 1) * 512], self.onescol_bf[:],
                             self.xbf_ap()[:, tt * 512:(tt + 1) * 512],
                             start=(c == 0), stop=(c == 7))
                self.sV.wait(t, vbase + 2 * c + 2)
                for tt in range(4):
                    ins = t.matmul(s2p[0:1, tt * 512:(tt + 1) * 512], self.onescol_bf[:],
                                   self.sqbf_ap()[:, tt * 512:(tt + 1) * 512],
                                   start=(c == 0), stop=(c == 7))
                ins.then_inc(self.sT.h, 8)
                self.sT.n += 8

        tdone = self.sT.n

        @b.vector
        def _(v):
            self.sT.wait(v, tdone)
            v.tensor_copy(self.s1_sb[:], s1p[:, :])
            v.tensor_copy(self.s2_sb[:], s2p[:, :])
            v.tensor_scalar_mul(self.mu_sb[:], self.s1_sb[:], 1.0 / D)
            v.tensor_scalar_mul(self.sc_sb[:], self.s2_sb[:], 1.0 / D)
            v.tensor_mul(self.sc2_sb[:], self.mu_sb[:], self.mu_sb[:])
            v.tensor_sub(self.sc_sb[:], self.sc_sb[:], self.sc2_sb[:])
            self.sV.inc(v.tensor_scalar_add(self.sc2_sb[:, :], self.sc_sb[:], EPS))

        @b.scalar
        def _(s):
            self.sV.wait(s)
            self.sSc.inc(s.activation(out=self.a1_sb[:], in_=self.sc2_sb[:], func=AF.Sqrt))

        @b.vector
        def _(v):
            self.sSc.wait(v)
            self.sV.inc(v.reciprocal(self.a1_sb[:], self.a1_sb[:]))

        if want_aa:
            @b.vector
            def _(v):
                v.reciprocal(self.aa_sb[:], self.sc2_sb[:])
                v.tensor_mul(self.aa_sb[:], self.aa_sb[:], self.sc_sb[:])
                self.sV.inc(v.tensor_scalar_add(self.aa_sb[:], self.aa_sb[:], EPS))

            @b.scalar
            def _(s):
                self.sV.wait(s)
                self.sSc.inc(s.activation(out=self.aa_sb[:], in_=self.aa_sb[:], func=AF.Sqrt))

            @b.vector
            def _(v):
                self.sSc.wait(v)
                v.reciprocal(self.aa_sb[:], self.aa_sb[:])
                self.sV.inc(v.tensor_mul(self.aa_sb[:], self.aa_sb[:], self.a1_sb[:]))
        self._close(pctx)

    def _bcast(self, dst_ps, src_sb, dep_sem, dep_val):
        b = self.block

        @b.tensor
        def _(t):
            dep_sem.wait(t, dep_val)
            for tt in range(4):
                ins = t.matmul(dst_ps[:, tt * 512:(tt + 1) * 512], self.onesrow_f[:, :],
                               src_sb[:, tt * 512:(tt + 1) * 512], start=True, stop=True)
            self.sBC.inc(ins)

    def emit_ln_a(self, l):
        b = self.block
        pctx = self._ctx()
        self.ln_scratch(pctx)
        self._stats(l, self.xdep, want_aa=(l == 0))
        b1 = self.ps("lnb1", [128, TLOC], ctx=pctx)
        b2 = self.ps("lnb2", [128, TLOC], ctx=pctx)
        self._bcast(b1, self.mu_sb, self.sV, self.sV.n)
        self._bcast(b2, self.aa_sb if l == 0 else self.a1_sb, self.sV, self.sV.n)
        bc1 = self.sBC.n

        if l == 0:
            @b.vector
            def _(v):
                self.sBC.wait(v, bc1)
                for c in range(8):
                    v.tensor_sub(self.tmp_sb[:], self.x_sb[:, c, :], b1[:, :])
                    v.tensor_mul(self.tmp2_sb[:], self.tmp_sb[:], b2[:, :])
                    ins = v.tensor_scalar(out=self.xn_sb[:, c, :], in0=self.tmp2_sb[:],
                                          scalar1=self.prm["l0_gamma"][:, c:c + 1],
                                          scalar2=self.prm["l0_beta"][:, c:c + 1],
                                          op0=ALU.mult, op1=ALU.add)
                self.sV.inc(ins)
            self._bcast(b2, self.a1_sb, self.sV, self.sV.n)
            bc2 = self.sBC.n

            @b.vector
            def _(v):
                self.sBC.wait(v, bc2)
                for c in range(8):
                    v.tensor_sub(self.tmp_sb[:], self.x_sb[:, c, :], b1[:, :])
                    v.tensor_mul(self.tmp2_sb[:], self.tmp_sb[:], b2[:, :])
                    ins = v.tensor_add(self.x_sb[:, c, :], self.x_sb[:, c, :], self.tmp2_sb[:])
                self.sV.inc(ins)
        else:
            @b.vector
            def _(v):
                self.sBC.wait(v, bc1)
                for c in range(8):
                    v.tensor_sub(self.tmp_sb[:], self.x_sb[:, c, :], b1[:, :])
                    ins = v.tensor_mul(self.xn_sb[:, c, :], self.tmp_sb[:], b2[:, :])
                self.sV.inc(ins)
        self._close(pctx)
        self.tap(f"xn{l}", lambda: self.xn_sb.ap(), [128, 8, TLOC], BF16)
        self.tap(f"xr{l}", lambda: self.x_sb.ap(), [128, 8, TLOC], F32)

    def emit_ln_b(self, l):
        self.xdep = (self.sV, self.sV.n)
        b = self.block
        pctx = self._ctx()
        self.ln_scratch(pctx)
        self._stats(l, self.xdep, want_aa=False)
        b1 = self.ps("lnb1b", [128, TLOC], ctx=pctx)
        b2 = self.ps("lnb2b", [128, TLOC], ctx=pctx)
        self._bcast(b1, self.mu_sb, self.sV, self.sV.n)
        self._bcast(b2, self.a1_sb, self.sV, self.sV.n)
        bc = self.sBC.n

        @b.vector
        def _(v):
            self.sBC.wait(v, bc)
            for c in range(8):
                v.tensor_sub(self.tmp_sb[:], self.x_sb[:, c, :], b1[:, :])
                ins = v.tensor_mul(self.xn_sb[:, c, :], self.tmp_sb[:], b2[:, :])
            self.sV.inc(ins)
        self._close(pctx)
        self.tap(f"xnb{l}", lambda: self.xn_sb.ap(), [128, 8, TLOC], BF16)

    def emit_qkv(self, l):
        b = self.block
        lyr = f"l{l}"
        pctx = self._ctx()
        self.qo_sb = self.sb("qo_sb", [128, NHP, TLOC], BF16, ctx=pctx)
        qsctx = self._ctx()
        cl_sb = self.sb("cl", [128, TLOC], F32, ctx=qsctx)
        sl_sb = self.sb("sl", [128, TLOC], F32, ctx=qsctx)
        wA = self.sb("wA", [128, 8, D], BF16, ctx=qsctx)
        wB = self.sb("wB", [128, 8, D], BF16, ctx=qsctx)
        kstage = self.sb("kstage", [128, 2, CH], BF16, ctx=qsctx)
        ropeb = self.sb("ropeb", [128, 2, CH], BF16, ctx=qsctx)
        ksq = self.sb("ksq", [128, 4, CH], BF16, ctx=qsctx)
        vstage = self.sb("vstage", [128, 2, 8, 130], BF16, ctx=qsctx)
        nrmk_sb = self.sb("nrmk", [1, 64], F32, ctx=qsctx)
        qpsctx = self._ctx()
        qkp = self.ps("qkp", [128, 2, 512], ctx=qpsctx)
        vp = self.ps("vp", [128, 2, 1024], ctx=qpsctx)
        nrmp = self.ps("nrmp", [1, 2, 512], ctx=qpsctx)

        sQP = self.sem(f"sQP{l}")
        sRP = self.sem(f"sRP{l}")
        sSQ = self.sem(f"sSQ{l}")
        sNM = self.sem(f"sNM{l}")
        sNR = self.sem(f"sNR{l}")
        sVP = self.sem(f"sVP{l}")
        sVE = self.sem(f"sVE{l}")

        xn_ready = self.sV.n

        @b.gpsimd
        def _(g):
            g.wait_ge(self.sV.h, xn_ready)
            self.sW.dma(g.dma_start(out=wA[:], in_=self.w_d[lyr + "_wq"].rearrange("(c p) o -> p c o", p=128)))
        w_q_ready = self.sW.n

        @b.sync
        def _(s):
            s.wait_ge(self.sV.h, xn_ready)
            self.sLD.dma(s.dma_start(out=cl_sb[:], in_=self.cl_d[:, :]))
            self.sLD.dma(s.dma_start(out=sl_sb[:], in_=self.sl_d[:, :]))
        ld_ready = self.sLD.n

        @b.gpsimd
        def _(g):
            g.wait_ge(self.sV.h, xn_ready)
            self.sW.dma(g.dma_start(out=wB[:], in_=self.w_d[lyr + "_wk"].rearrange("(c p) o -> p c o", p=128)))
        self.w_k_ready = self.sW.n

        for which in (0, 1):
            wsb = wA if which == 0 else wB
            bias = self.prm["l0_bq" if which == 0 else "l0_bk"] if l == 0 else None
            aS = self.aQ if which == 0 else self.aK
            abase = [aS[0].n, aS[1].n]
            kwbase = [self.kW[0].n, self.kW[1].n]
            qp0, rp0, sq0, nm0, nr0 = sQP.n, sRP.n, sSQ.n, sNM.n, sNR.n

            wrdy = w_q_ready if which == 0 else self.w_k_ready

            @b.tensor
            def _(t, wsb=wsb, which=which, qp0=qp0, rp0=rp0, sq0=sq0, nm0=nm0,
                  nr0=nr0, wrdy=wrdy):
                t.wait_ge(self.sW.h, wrdy)
                t.wait_ge(self.sV.h, xn_ready)
                t.wait_ge(self.sLD.h, ld_ready)
                for k in range(32):
                    hp, tt = k // 4, k % 4
                    slot = k % 2
                    if rp0 + k - 1 > 0:
                        t.wait_ge(sRP.h, rp0 + k - 1)
                    for dc in range(8):
                        ins = t.matmul(qkp[:, slot, :], wsb[:, dc, hp * 128:(hp + 1) * 128],
                                       self.xn_sb[:, dc, tt * 512:(tt + 1) * 512],
                                       start=(dc == 0), stop=(dc == 7))
                    ins.then_inc(sQP.h, 1)
                    if k >= 4:
                        j = k - 4
                        t.wait_ge(sSQ.h, sq0 + j + 1)
                        if j >= 2:
                            t.wait_ge(sNR.h, nr0 + j - 1)
                        t.matmul(nrmp[0:1, j % 2, :], self.onescol_bf[:], ksq[:, j % 4, :],
                                 start=True, stop=True).then_inc(sNM.h, 1)
                for j in range(28, 32):
                    t.wait_ge(sSQ.h, sq0 + j + 1)
                    t.wait_ge(sNR.h, nr0 + j - 1)
                    t.matmul(nrmp[0:1, j % 2, :], self.onescol_bf[:], ksq[:, j % 4, :],
                             start=True, stop=True).then_inc(sNM.h, 1)
            sQP.n = qp0 + 32
            sNM.n = nm0 + 32

            @b.vector
            def _(v, which=which, bias=bias, aS=aS, abase=abase, kwbase=kwbase,
                  qp0=qp0, sq0=sq0, nm0=nm0, ckstage=kstage):
                for k in range(32):
                    hp, tt = k // 4, k % 4
                    slot = k % 2
                    v.wait_ge(sQP.h, qp0 + k + 1)
                    if k >= 2:
                        v.wait_ge(aS[slot].h, abase[slot] + 64 * (k // 2))
                    elif which == 1:
                        v.wait_ge(self.aQ[slot].h, self.aQ[slot].n)
                    if nm0 + k - 3 > 0:
                        v.wait_ge(sNM.h, nm0 + k - 3)
                    destA = (self.qo_sb[:, hp, tt * 512:(tt + 1) * 512] if which == 0
                             else ckstage[:, slot, :])
                    if which == 1 and k >= 2:
                        v.wait_ge(self.kW[slot].h, kwbase[slot] + 16 * (k // 2))
                    bias_ap = bias[:, hp:hp + 1] if bias is not None else 0.0
                    v.scalar_tensor_tensor(out=destA, in0=qkp[:, slot, :], scalar=bias_ap,
                                           in1=cl_sb[:, tt * 512:(tt + 1) * 512],
                                           op0=ALU.add, op1=ALU.mult)
                    v.scalar_tensor_tensor(out=ropeb[:, slot, :], in0=qkp[:, slot, :],
                                           scalar=bias_ap,
                                           in1=sl_sb[:, tt * 512:(tt + 1) * 512],
                                           op0=ALU.add, op1=ALU.mult).then_inc(sRP.h, 1)
                    v.wait_ge(aS[slot].h, abase[slot] + 64 * (k // 2 + 1))
                    v.tensor_mul(ksq[:, k % 4, :], destA, destA).then_inc(sSQ.h, 1)
                    if k >= 4:
                        j = k - 4
                        v.wait_ge(sNM.h, nm0 + j + 1)
                        v.reduce_max(nrmk_sb[0:1, which * 32 + j:which * 32 + j + 1],
                                     nrmp[0:1, j % 2, :], axis=mybir.AxisListType.X).then_inc(sNR.h, 1)
                for j in range(28, 32):
                    v.wait_ge(sNM.h, nm0 + j + 1)
                    v.reduce_max(nrmk_sb[0:1, which * 32 + j:which * 32 + j + 1],
                                 nrmp[0:1, j % 2, :], axis=mybir.AxisListType.X).then_inc(sNR.h, 1)
            sRP.n = rp0 + 32
            sSQ.n = sq0 + 32
            sNR.n = nr0 + 32

            @b.gpsimd
            def _(g, which=which, aS=aS, abase=abase, rp0=rp0, kwbase=kwbase,
                  ckstage=kstage):
                for k in range(32):
                    hp, tt = k // 4, k % 4
                    slot = k % 2
                    g.wait_ge(sRP.h, rp0 + k + 1)
                    destA = (self.qo_sb[:, hp, tt * 512:(tt + 1) * 512] if which == 0
                             else ckstage[:, slot, :])
                    for lo in (0, 64):
                        g.dma_start(out=destA[lo:lo + 32, :], in_=ropeb[lo + 32:lo + 64, slot, :],
                                    accum_op=ALU.add).then_inc(aS[slot].h, 16)
                        g.dma_start(out=destA[lo + 32:lo + 64, :], in_=ropeb[lo:lo + 32, slot, :],
                                    accum_op=ALU.add).then_inc(aS[slot].h, 16)
                    if which == 1:
                        g.wait_ge(aS[slot].h, abase[slot] + 64 * (k // 2 + 1))
                        with g.If_eq(self.role_g, 0):
                            g.dma_start(out=self.shk_d[l][ROLE_CHUNKS[0][tt], hp, :, :],
                                        in_=ckstage[:, slot, :]).then_inc(self.kW[slot].h, 16)
                        with g.Else():
                            g.dma_start(out=self.shk_d[l][ROLE_CHUNKS[1][tt], hp, :, :],
                                        in_=ckstage[:, slot, :]).then_inc(self.kW[slot].h, 16)
                        self.kW[slot].n += 16
            aS[0].n = abase[0] + 64 * 16
            aS[1].n = abase[1] + 64 * 16



        @b.gpsimd
        def _(g):
            g.wait_ge(sQP.h, sQP.n - 32)
            self.sW.dma(g.dma_start(out=wA[:], in_=self.w_d[lyr + "_wv"].rearrange("(c p) o -> p c o", p=128)))
        w_v_ready = self.sW.n

        vp0, ve0 = sVP.n, sVE.n
        vwbase = [self.vW[0].n, self.vW[1].n]

        @b.vector
        def _(v):
            v.memset(vstage[:, :, :, 64:65], 1.0)
            v.memset(vstage[:, :, :, 129:130], 1.0).then_inc(sVE.h, 1)

        @b.tensor
        def _(t):
            t.wait_ge(self.sW.h, w_v_ready)
            t.wait_ge(self.sG.h, self.init_gate)
            for k in range(16):
                slot = k % 2
                if k >= 2:
                    t.wait_ge(sVE.h, ve0 + 1 + (k - 2) + 1)
                for dc in range(8):
                    t.matmul(vp[:, slot, 0:512], self.xn_sb[:, dc, k * 128:(k + 1) * 128],
                             wA[:, dc, 0:512], start=(dc == 0), stop=(dc == 7 and l != 0))
                    ins = t.matmul(vp[:, slot, 512:1024], self.xn_sb[:, dc, k * 128:(k + 1) * 128],
                                   wA[:, dc, 512:1024], start=(dc == 0), stop=(dc == 7 and l != 0))
                if l == 0:
                    t.matmul(vp[:, slot, 0:512], self.onesrow_bf[:, :],
                             self.prm["l0_bv"][0:1, 0:512], start=False, stop=True)
                    ins = t.matmul(vp[:, slot, 512:1024], self.onesrow_bf[:, :],
                                   self.prm["l0_bv"][0:1, 512:1024], start=False, stop=True)
                ins.then_inc(sVP.h, 1)
        sVP.n = vp0 + 16

        @b.vector
        def _(v):
            for k in range(16):
                slot = k % 2
                v.wait_ge(sVP.h, vp0 + k + 1)
                if k >= 2:
                    v.wait_ge(self.vW[slot].h, vwbase[slot] + 16 * (k // 2))
                src = vp.ap()[:, slot, :].rearrange("p (h t d) -> p h t d", h=8, t=2)
                v.tensor_copy(vstage[:, slot, :, 0:64], src[:, :, 0, :])
                v.tensor_copy(vstage[:, slot, :, 65:129], src[:, :, 1, :]).then_inc(sVE.h, 1)
        sVE.n = ve0 + 17

        @b.gpsimd
        def _(g):
            for k in range(16):
                slot = k % 2
                lc, kt = k // 4, k % 4
                g.wait_ge(sVE.h, ve0 + 1 + k + 1)
                with g.If_eq(self.role_g, 0):
                    g.dma_start(out=self.shv_d[l][ROLE_CHUNKS[0][lc], :, kt, :, :].transpose([1, 0, 2]),
                                in_=vstage[:, slot, :, :]).then_inc(self.vW[slot].h, 16)
                with g.Else():
                    g.dma_start(out=self.shv_d[l][ROLE_CHUNKS[1][lc], :, kt, :, :].transpose([1, 0, 2]),
                                in_=vstage[:, slot, :, :]).then_inc(self.vW[slot].h, 16)
                self.vW[slot].n += 16

        @b.vector
        def _(v):
            v.wait_ge(sNR.h, sNR.n)
            v.reduce_max(self.mynorm_sb[0:1, 1:2], nrmk_sb[0:1, 32:64], axis=mybir.AxisListType.X)
            self.sV.inc(v.reduce_max(self.mynorm_sb[0:1, 0:1], nrmk_sb[0:1, 0:32], axis=mybir.AxisListType.X))
        self._close(qpsctx)
        self._close(qsctx)
        self.pctx_qkv = pctx
        self.tap(f"q{l}", lambda: self.qo_sb.ap(), [128, NHP, TLOC], BF16)
        self.tap(f"norm{l}", lambda: self.mynorm_sb.ap(), [1, 4], F32)

    @staticmethod
    def attn_passes(role, g):
        chunks = ROLE_CHUNKS[role]
        la, lb = GROUPS[g]
        a, bg = chunks[la], chunks[lb]
        out = [("D", la, a, 0), ("D", lb, bg, 1)]
        for kg in range(a):
            out.append(("O", la, kg, 0))
        for kg in range(bg):
            out.append(("O", lb, kg, 1))
        assert len(out) == 9
        return out

    def emit_xchg(self, l):
        b = self.block
        norms_done = self.sV.n

        @b.gpsimd
        def _(g):
            for s in (self.kW + self.vW + self.aQ + self.aK):
                g.wait_ge(s.h, s.n)
            self.sG.wait(g)
            g.wait_ge(self.sV.h, norms_done)
            with g.If_eq(self.role_g, 0):
                g.dma_start(out=self.shnorm_d[l][0, :, 0:2], in_=self.mynorm_sb[:, 0:2]).then_inc(self.sFl.h, 16)
            with g.Else():
                g.dma_start(out=self.shnorm_d[l][1, :, 0:2], in_=self.mynorm_sb[:, 0:2]).then_inc(self.sFl.h, 16)
            self.sFl.n += 16
            self.sFl.wait(g)
            if not hasattr(self, "fl_val"):
                self.fl_val = g.alloc_register("fl_val")
                self.fl_cur = g.alloc_register("fl_cur")
                self.fl_cnt = g.alloc_register("fl_cnt")
                g.reg_mov(self.fl_cnt, 0)
            g.load(self.fl_val, self.nonce_sb[0:1, 0:1])
            g.reg_add(self.fl_val, self.fl_val, l + 1)
            g.store(self.flag_sb[0:1, 0:1], self.fl_val)
            with g.If_eq(self.role_g, 0):
                g.dma_start(out=self.shflag_d[l][0, :, 0:1], in_=self.flag_sb[0:1, 0:1]).then_inc(self.sFl.h, 16)
            with g.Else():
                g.dma_start(out=self.shflag_d[l][1, :, 0:1], in_=self.flag_sb[0:1, 0:1]).then_inc(self.sFl.h, 16)
            self.sFl.n += 16
            g.reg_mov(self.fl_cur, 0)
            with g.If_eq(self.role_g, 0):
                with g.While(lambda: _rv(ex.binary_op(_rv(self.fl_cur), _rv(self.fl_val), ALU.not_equal))):
                    g.reg_add(self.fl_cnt, self.fl_cnt, 16)
                    g.dma_start(out=self.pflag_sb[0:1, 0:1], in_=self.shflag_d[l][1, :, 0:1]).then_inc(self.sPol.h, 16)
                    g.wait_ge(self.sPol.h, self.fl_cnt)
                    g.load(self.fl_cur, self.pflag_sb[0:1, 0:1])
                g.dma_start(out=self.pnorm_sb[:, 0:2], in_=self.shnorm_d[l][1, :, 0:2]).then_inc(self.sPr.h, 16)
            with g.Else():
                with g.While(lambda: _rv(ex.binary_op(_rv(self.fl_cur), _rv(self.fl_val), ALU.not_equal))):
                    g.reg_add(self.fl_cnt, self.fl_cnt, 16)
                    g.dma_start(out=self.pflag_sb[0:1, 0:1], in_=self.shflag_d[l][0, :, 0:1]).then_inc(self.sPol.h, 16)
                    g.wait_ge(self.sPol.h, self.fl_cnt)
                    g.load(self.fl_cur, self.pflag_sb[0:1, 0:1])
                g.dma_start(out=self.pnorm_sb[:, 0:2], in_=self.shnorm_d[l][0, :, 0:2]).then_inc(self.sPr.h, 16)
            self.sPr.n += 16
            g.wait_ge(self.sPr.h, self.sPr.n)
            g.nop().then_inc(self.sG.h, 1)
            self.sG.n += 1
        partner_ready = self.sG.n

        pctx = self._ctx()
        bp = self.ps("bp", [128, 1], ctx=pctx)

        @b.vector
        def _(v):
            g = None
            v.wait_ge(self.sG.h, partner_ready)
            v.tensor_max(self.bscr_sb[0:1, 0:1], self.mynorm_sb[0:1, 1:2], self.pnorm_sb[0:1, 1:2])
            self.sV.inc(v.tensor_mul(self.bscr_sb[0:1, 1:2], self.bscr_sb[0:1, 0:1], self.mynorm_sb[0:1, 0:1]))

        @b.scalar
        def _(s):
            self.sV.wait(s)
            self.sSc.inc(s.activation(out=self.bscr_sb[0:1, 2:3], in_=self.bscr_sb[0:1, 1:2], func=AF.Sqrt))

        @b.vector
        def _(v):
            self.sSc.wait(v)
            self.sV.inc(v.tensor_scalar_mul(self.bscr_sb[0:1, 3:4], self.bscr_sb[0:1, 2:3], -0.127551))

        @b.tensor
        def _(t):
            self.sV.wait(t)
            t.wait_ge(self.sG.h, self.init_gate)
            self.sT.inc(t.matmul(bp[:, :], self.onesrow_f[:, :], self.bscr_sb[0:1, 3:4], start=True, stop=True))

        @b.vector
        def _(v):
            self.sT.wait(v)
            self.sV.inc(v.tensor_copy(self.negb_sb[:], bp[:, :]))
        self._close(pctx)
        self.b_ready = self.sV.n

    UNIT_ORDER = [(0, 0), (1, 0), (0, 1), (1, 1)]

    def emit_attn(self, l):
        b = self.block
        pctx0 = self._ctx()
        woctx = pctx0
        pctx = self._ctx()
        psctx = self._ctx()
        self.pctx_wo = woctx
        wo_sb = self.sb("wo", [128, 8, D], BF16, ctx=woctx)
        self.wo_sb = wo_sb
        kstg = self.sb("kstg", [128, 2, CH], BF16, ctx=pctx)
        vstg = self.sb("vstg", [128, 2, 4, 130], BF16, ctx=pctx)
        pt_sb = self.sb("pt", [128, 2, 1024], BF16, ctx=pctx)
        rc_sb = self.sb("rc", [1, 2048], F32, ctx=pctx)
        bcast_sb = self.sb("bcast", [64, 2048], F32, ctx=pctx)
        st = self.ps("st", [128, 2, 1024], ctx=psctx)
        acc = [self.ps(f"acc{h}", [65, 1024], ctx=psctx) for h in range(2)]
        self.pctx_attn = pctx
        self.psctx_attn = psctx

        sIn = self.sem(f"sIn{l}")
        stS = [self.sem(f"stS{l}_{i}") for i in range(2)]
        uQK = self.sem(f"uQK{l}")
        uEX = self.sem(f"uEX{l}")
        uMK = self.sem(f"uMK{l}")
        uPV = self.sem(f"uPV{l}")
        vRC = self.sem(f"vRC{l}")
        tBC = self.sem(f"tBC{l}")
        vEV = self.sem(f"vEV{l}")
        lyr = f"l{l}"
        NU = 16 * 36

        @b.gpsimd
        def _(g):
            self.sW.dma(g.dma_start(out=wo_sb[:], in_=self.w_d[lyr + "_wo"].rearrange("(c p) o -> p c o", p=128)))

        @b.gpsimd
        def _(g):
            for hp in range(NHP):
                for gi in range(2):
                    idx = hp * 2 + gi
                    for p in range(9):
                        P = idx * 9 + p
                        slot = P % 2
                        if 4 * (P - 1) > 0:
                            g.wait_ge(uPV.h, 4 * (P - 1))
                        for role in range(2):
                            kg = self.attn_passes(role, gi)[p][2]
                            with (g.If_eq(self.role_g, 0) if role == 0 else g.Else()):
                                g.dma_start(out=kstg[:, slot, :],
                                            in_=self.shk_d[l][kg, hp, :, :]).then_inc(stS[slot].h, 16)
                                g.dma_start(out=vstg[:, slot, :, :],
                                            in_=self.shv_d[l][kg, hp, :, :, :].transpose([1, 0, 2])).then_inc(stS[slot].h, 16)
                        stS[slot].n += 32

        qk_ready = [(s, s.n) for s in (self.aQ + self.aK)]

        def unit_descs(role):
            descs = []
            for hp in range(NHP):
                for gi in range(2):
                    idx = hp * 2 + gi
                    passes = self.attn_passes(role, gi)
                    for p in range(9):
                        typ, qloc, kg, sel = passes[p]
                        for u, (head, ktp) in enumerate(self.UNIT_ORDER):
                            descs.append(dict(U=(idx * 9 + p) * 4 + u, idx=idx, hp=hp,
                                              gi=gi, p=p, P=idx * 9 + p, typ=typ,
                                              qloc=qloc, sel=sel, head=head, ktp=ktp))
            return descs

        def emit_qk(t, d):
            slot2 = d["U"] % 2
            if d["U"] - 1 > 0:
                t.wait_ge(uEX.h, d["U"] - 1)
            if d["U"] % 4 == 0:
                t.wait_ge(stS[d["P"] % 2].h, 32 * (d["P"] // 2 + 1))
                if d["p"] == 0 and d["idx"] > 0:
                    t.wait_ge(vEV.h, d["idx"])
            for ktl in range(2):
                kt = d["ktp"] * 2 + ktl
                h = d["head"]
                ins = t.matmul(st[:, slot2, ktl * 512:(ktl + 1) * 512],
                               kstg[h * 64:(h + 1) * 64, d["P"] % 2, kt * 128:(kt + 1) * 128],
                               self.qo_sb[h * 64:(h + 1) * 64, d["hp"],
                                          d["qloc"] * 512:(d["qloc"] + 1) * 512],
                               start=True, stop=True, tile_position=(h * 64, 0))
            ins.then_inc(uQK.h, 1)

        def emit_pv(t, d):
            slot2 = d["U"] % 2
            if d["typ"] == "D":
                t.wait_ge(uMK.h, self.mk_count(d["U"] + 1))
            else:
                t.wait_ge(uEX.h, d["U"] + 1)
            first = (d["typ"] == "D" and d["ktp"] == 0)
            h = d["head"]
            for ktl in range(2):
                kt = d["ktp"] * 2 + ktl
                qoff = kt * 128 if d["typ"] == "D" else 0
                ins = t.matmul(acc[h][:, d["sel"] * 512 + qoff:(d["sel"] + 1) * 512],
                               vstg[:, d["P"] % 2, kt, h * 65:(h + 1) * 65],
                               pt_sb[:, slot2, ktl * 512 + qoff:(ktl + 1) * 512],
                               start=(first and ktl == 0), stop=False, skip_group_check=True)
            ins.then_inc(uPV.h, 1)

        def emit_evac_mms(t, idx):
            t.wait_ge(vRC.h, idx + 1)
            t.wait_ge(uEX.h, (idx + 1) * 36)
            for head in range(2):
                for sel in range(2):
                    ins = t.matmul(st[:, head, sel * 512:(sel + 1) * 512], self.onesrow_f[:, :],
                                   rc_sb[0:1, head * 1024 + sel * 512:head * 1024 + (sel + 1) * 512],
                                   start=True, stop=True)
            ins.then_inc(tBC.h, 1)

        def emit_units_tensor(t, role):
            descs = unit_descs(role)
            pend = None
            for d in descs:
                if pend is not None and d["idx"] != pend["idx"]:
                    emit_pv(t, pend)
                    emit_evac_mms(t, pend["idx"])
                    pend = None
                emit_qk(t, d)
                if pend is not None:
                    emit_pv(t, pend)
                pend = d
            emit_pv(t, pend)
            emit_evac_mms(t, pend["idx"])

        @b.tensor
        def _(t):
            t.wait_ge(self.sG.h, self.init_gate)
            t.wait_ge(sIn.h, 1)
            for s2, v2 in qk_ready:
                t.wait_ge(s2.h, v2)
            with t.If_eq(self.role_t, 0):
                emit_units_tensor(t, 0)
            with t.Else():
                emit_units_tensor(t, 1)
        uQK.n = NU
        uPV.n = NU
        tBC.n = 16

        @b.scalar
        def _(s):
            s.wait_ge(self.sV.h, self.b_ready)
            for U in range(NU):
                slot2 = U % 2
                s.wait_ge(uQK.h, U + 1)
                if U - 1 > 0:
                    s.wait_ge(uPV.h, U - 1)
                s.activation(out=pt_sb[:, slot2, :], in_=st[:, slot2, :], func=AF.Exp,
                             bias=self.negb_sb[:, :], scale=0.125).then_inc(uEX.h, 1)
        uEX.n = NU

        @b.vector
        def _(v):
            v.wait_ge(self.sG.h, self.init_gate)
            v.memset(st.ap()[:, :, :], 0.0).then_inc(sIn.h, 1)
            for hp in range(NHP):
                for gi in range(2):
                    idx = hp * 2 + gi
                    la, lb = GROUPS[gi]
                    for p in range(2):
                        for u, (head, ktp) in enumerate(self.UNIT_ORDER):
                            U = (idx * 9 + p) * 4 + u
                            slot2 = U % 2
                            v.wait_ge(uEX.h, U + 1)
                            for ktl in range(2):
                                kt = ktp * 2 + ktl
                                off = ktl * 512 + kt * 128
                                ins = v.tensor_mul(pt_sb[:, slot2, off:off + 128],
                                                   pt_sb[:, slot2, off:off + 128], self.tri_sb[:, :])
                            ins.then_inc(uMK.h, 1)
                    v.wait_ge(uPV.h, (idx + 1) * 36)
                    for head in range(2):
                        ins = v.reciprocal(rc_sb[0:1, head * 1024:(head + 1) * 1024],
                                           acc[head][64:65, 0:1024])
                    ins.then_inc(vRC.h, 1)
                    v.wait_ge(tBC.h, idx + 1)
                    v.tensor_copy(bcast_sb[:, :], st.ap()[0:64, :, :].rearrange("p a b -> p (a b)"))
                    for head in range(2):
                        for sel, lc in ((0, la), (1, lb)):
                            ins = v.tensor_mul(self.qo_sb[head * 64:(head + 1) * 64, hp,
                                                          lc * 512:(lc + 1) * 512],
                                               acc[head][0:64, sel * 512:(sel + 1) * 512],
                                               bcast_sb[:, head * 1024 + sel * 512:head * 1024 + (sel + 1) * 512])
                    ins.then_inc(vEV.h, 1)
        uMK.n = 16 * 8
        vRC.n = 16
        vEV.n = 16
        self.attn_sems = (uPV, vEV)
        self.vEV = vEV
        self._close(psctx)
        self._close(pctx)
        self.tap(f"ao{l}", lambda: self.qo_sb.ap(), [128, NHP, TLOC], BF16)

    @staticmethod
    def mk_count(U):
        full, rem = divmod(U, 36)
        return full * 8 + min(rem, 8)

    def emit_wo(self, l):
        b = self.block
        pctx = self._ctx()
        wop = self.ps("wop", [128, 2, 512], ctx=pctx)
        sWoT = self.sem(f"sWoT{l}")
        sWoV = self.sem(f"sWoV{l}")
        self.sWoT = sWoT
        uPV, vEV = self.attn_sems
        wo_sb = self.wo_sb
        wo_ready = self.sW.n

        @b.tensor
        def _(t):
            t.wait_ge(vEV.h, 16)
            t.wait_ge(self.sW.h, wo_ready)
            for k in range(32):
                dc, tt = k // 4, k % 4
                slot = k % 2
                if k >= 2:
                    t.wait_ge(sWoV.h, k - 1)
                for hp in range(NHP):
                    ins = t.matmul(wop[:, slot, :], wo_sb[:, hp, dc * 128:(dc + 1) * 128],
                                   self.qo_sb[:, hp, tt * 512:(tt + 1) * 512],
                                   start=(hp == 0), stop=(hp == 7))
                ins.then_inc(sWoT.h, 1)
        sWoT.n = 32

        @b.vector
        def _(v):
            for k in range(32):
                dc, tt = k // 4, k % 4
                slot = k % 2
                v.wait_ge(sWoT.h, k + 1)
                bias_ap = self.prm["l0_bo"][:, dc:dc + 1] if l == 0 else 0.0
                ins = v.scalar_tensor_tensor(out=self.x_sb[:, dc, tt * 512:(tt + 1) * 512],
                                             in0=wop[:, slot, :], scalar=bias_ap,
                                             in1=self.x_sb[:, dc, tt * 512:(tt + 1) * 512],
                                             op0=ALU.add, op1=ALU.add)
                ins.then_inc(sWoV.h, 1)
        sWoV.n = 32
        self._close(pctx)
        self._close(self.pctx_wo)
        self._close(self.pctx_qkv)
        self.tap(f"xw{l}", lambda: self.x_sb.ap(), [128, 8, TLOC], F32)

    def emit_ffn(self):
        b = self.block
        pctx = self._ctx()
        wf_g = self.sb("wf_g", [128, 3, 8, 512], BF16, ctx=pctx)
        wf_u = self.sb("wf_u", [128, 3, 8, 512], BF16, ctx=pctx)
        wf_d = self.sb("wf_d", [128, 3, 4, 1024], BF16, ctx=pctx)
        silu_sb = self.sb("silu", [128, 1024], F32, ctx=pctx)
        act_sb = self.sb("act", [128, 3, 4, 256], BF16, ctx=pctx)
        pg = self.ps("pg", [128, 4, 256], ctx=pctx)
        pu = self.ps("pu", [128, 4, 256], ctx=pctx)
        pd = self.ps("pd", [128, 8, 256], ctx=pctx)
        sFG = self.sem("sFG")
        sFA = self.sem("sFA")
        sFA2 = self.sem("sFA2")
        sFD = self.sem("sFD")
        sFE = self.sem("sFE")
        sFW = self.sem("sFW")
        uPV, vEV = self.attn_sems
        gate_v = self.sV.n

        lnb_done = self.sV.n

        @b.gpsimd
        def _(g):
            g.wait_ge(uPV.h, 16 * 36)
            g.wait_ge(self.sWoT.h, 32)
            g.wait_ge(self.sV.h, lnb_done)

        def load_wave(g, w):
            for i in range(3):
                hcg = w * 3 + i
                sFW.dma(g.dma_start(out=wf_g[:, i, :, :],
                                    in_=self.w_d["l0_wg"][:, hcg * 512:(hcg + 1) * 512]
                                    .rearrange("(c p) h -> p c h", p=128)))
                sFW.dma(g.dma_start(out=wf_u[:, i, :, :],
                                    in_=self.w_d["l0_wu"][:, hcg * 512:(hcg + 1) * 512]
                                    .rearrange("(c p) h -> p c h", p=128)))
                sFW.dma(g.dma_start(out=wf_d[:, i, :, :],
                                    in_=self.w_d["l0_wd"][hcg * 512:(hcg + 1) * 512, :]
                                    .rearrange("(s p) o -> p s o", p=128)))

        wave_w_ready = []

        @b.gpsimd
        def _(g):
            for w in range(2):
                if w == 1:
                    g.wait_ge(sFD.h, 24)
                load_wave(g, w)
                wave_w_ready.append(sFW.n)

        @b.tensor
        def _(t):
            t.wait_ge(self.sV.h, gate_v)

            def down(mp):
                wp, kp, ip = mp // 24, (mp // 3) % 8, mp % 3
                t.wait_ge(sFA2.h, mp + 1)
                if ip == 0:
                    kkp = wp * 8 + kp
                    if kkp >= 1:
                        t.wait_ge(sFE.h, kkp)
                for hs in range(4):
                    for dco in range(8):
                        ins = t.matmul(pd[:, dco, :], wf_d[:, ip, hs, dco * 128:(dco + 1) * 128],
                                       act_sb[:, ip, hs, :],
                                       start=(ip == 0 and hs == 0 and dco % 2 == 0),
                                       stop=False, skip_group_check=True)
                ins.then_inc(sFD.h, 1)

            for m in range(48):
                w = m // 24
                if m % 24 == 0:
                    if m >= 1:
                        down(m - 1)
                    t.wait_ge(sFW.h, wave_w_ready[w])
                if m >= 1:
                    t.wait_ge(sFA2.h, m)
                i = m % 3
                k = (m // 3) % 8
                ts = slice(k * 256, (k + 1) * 256)
                for hs in range(4):
                    for dc in range(8):
                        t.matmul(pg[:, hs, :], wf_g[:, i, dc, hs * 128:(hs + 1) * 128],
                                 self.xn_sb[:, dc, ts], start=(dc == 0), stop=(dc == 7))
                for hs in range(4):
                    for dc in range(8):
                        ins = t.matmul(pu[:, hs, :], wf_u[:, i, dc, hs * 128:(hs + 1) * 128],
                                       self.xn_sb[:, dc, ts], start=(dc == 0), stop=(dc == 7))
                ins.then_inc(sFG.h, 1)
                if m >= 1 and m % 24 != 0:
                    down(m - 1)
            down(47)
        sFG.n = 48
        sFD.n = 48

        @b.scalar
        def _(s):
            for m in range(48):
                s.wait_ge(sFG.h, m + 1)
                s.activation(out=silu_sb[:, :], in_=pg.ap()[:, :, :].rearrange("p a b -> p (a b)"),
                             func=AF.Silu, scale=1.0 / 127.0).then_inc(sFA.h, 1)
        sFA.n = 48

        @b.vector
        def _(v):
            for m in range(48):
                i = m % 3
                v.wait_ge(sFA.h, m + 1)
                if m >= 3:
                    v.wait_ge(sFD.h, m - 2)
                v.scalar_tensor_tensor(out=act_sb[:, i, :, :],
                                       in0=pu.ap()[:, :, :], scalar=1.0 / 127.0,
                                       in1=silu_sb.ap().rearrange("p (a b) -> p a b", a=4),
                                       op0=ALU.mult, op1=ALU.mult).then_inc(sFA2.h, 1)
                if i == 2:
                    kk = m // 3
                    k = kk % 8
                    v.wait_ge(sFD.h, (kk + 1) * 3)
                    ts = slice(k * 256, (k + 1) * 256)
                    for dco in range(8):
                        ins = v.scalar_tensor_tensor(out=self.x_sb[:, dco, ts], in0=pd[:, dco, :],
                                                     scalar=1.0 / 127.0, in1=self.x_sb[:, dco, ts],
                                                     op0=ALU.mult, op1=ALU.add)
                    ins.then_inc(sFE.h, 1)
        sFA2.n = 48
        sFE.n = 16
        self._close(pctx)
        self.xdep = (sFE, 16)
        self.tap("xf", lambda: self.x_sb.ap(), [128, 8, TLOC], F32)

    def emit_router(self):
        b = self.block
        pctx = self._ctx()
        self.moe_ctx = pctx
        self.gate_fm = self.sb("gate_fm", [16, TLOC], F32, ctx=pctx)
        rctx = self._ctx()
        rw_sb = self.sb("rw", [128, 8, 16], BF16, ctx=rctx)
        pe_sb = self.sb("pe", [128, 16], F32, ctx=rctx)
        sume_sb = self.sb("sume", [128, 1], F32, ctx=rctx)
        rsume_sb = self.sb("rsume", [128, 1], F32, ctx=rctx)
        rjunk_sb = self.sb("rjunk", [128, 1], F32, ctx=rctx)
        m8_sb = self.sb("m8", [128, 8], F32, ctx=rctx)
        prb_sb = self.sb("prb", [128, 2, 16], F32, ctx=rctx)
        prp = self.ps("prp", [128, 2, 512], ctx=rctx)
        gtp = self.ps("gtp", [16, 2, 512], ctx=rctx)
        sRT = self.sem("sRT")
        sRS = self.sem("sRS")
        sRMu = self.sem("sRMu")
        sRRc = self.sem("sRRc")
        sRGt = self.sem("sRGt")
        sRTr = self.sem("sRTr")
        sRCp = self.sem("sRCp")
        xn_ready = self.sV.n

        @b.gpsimd
        def _(g):
            g.wait_ge(self.sV.h, xn_ready)
            self.sW.dma(g.dma_start(out=rw_sb[:], in_=self.w_d["l1_rw"].rearrange("(c p) e -> p c e", p=128)))
        rw_ready = self.sW.n

        @b.tensor
        def _(t):
            t.wait_ge(self.sW.h, rw_ready)
            t.wait_ge(self.sV.h, xn_ready)
            t.wait_ge(self.sG.h, self.init_gate)
            for k in range(16):
                slot = k % 2
                if k >= 2:
                    t.wait_ge(sRS.h, k - 1)
                for dc in range(8):
                    ins = t.matmul(prp[:, slot, 0:16], self.xn_sb[:, dc, k * 128:(k + 1) * 128],
                                   rw_sb[:, dc, :], start=(dc == 0), stop=(dc == 7))
                ins.then_inc(sRT.h, 1)
                if k >= 1:
                    t.wait_ge(sRGt.h, k)
                    if k >= 3:
                        t.wait_ge(sRCp.h, k - 2)
                    t.transpose(gtp[:, (k - 1) % 2, 0:128], prb_sb[:, (k - 1) % 2, :],
                                self.ident_f[:, :]).then_inc(sRTr.h, 1)
            t.wait_ge(sRGt.h, 16)
            t.wait_ge(sRCp.h, 14)
            t.transpose(gtp[:, 15 % 2, 0:128], prb_sb[:, 15 % 2, :],
                        self.ident_f[:, :]).then_inc(sRTr.h, 1)
        sRT.n = 16
        sRTr.n = 16

        @b.scalar
        def _(s):
            for k in range(16):
                slot = k % 2
                s.wait_ge(sRT.h, k + 1)
                if k >= 1:
                    s.wait_ge(sRMu.h, k)
                    s.wait_ge(sRRc.h, k)
                s.activation(out=pe_sb[:, :], in_=prp[:, slot, 0:16], func=AF.Exp,
                             bias=0.0, scale=1.0, accum_out=sume_sb[:, :]).then_inc(sRS.h, 1)
        sRS.n = 16

        @b.vector
        def _(v):
            for k in range(16):
                slot = k % 2
                v.wait_ge(sRS.h, k + 1)
                v.reciprocal(rsume_sb[:, :], sume_sb[:, :]).then_inc(sRRc.h, 1)
                if k >= 1:
                    v.scalar_tensor_tensor(out=prb_sb[:, 1 - slot, :], in0=prb_sb[:, 1 - slot, :],
                                           scalar=m8_sb[:, 3:4], in1=prb_sb[:, 1 - slot, :],
                                           op0=ALU.is_ge, op1=ALU.mult).then_inc(sRGt.h, 1)
                else:
                    v.tensor_copy(rjunk_sb[:, :], sume_sb[:, :])
                v.tensor_scalar_mul(prb_sb[:, slot, :], pe_sb[:, :], rsume_sb[:, 0:1]).then_inc(sRMu.h, 1)
                if k >= 2:
                    v.wait_ge(sRTr.h, k - 1)
                    v.tensor_copy(self.gate_fm[:, (k - 2) * 128:(k - 1) * 128],
                                  gtp[:, k % 2, 0:128]).then_inc(sRCp.h, 1)
                else:
                    v.tensor_copy(rjunk_sb[:, :], rsume_sb[:, :])
                v.max(out=m8_sb[:, :], in_=prb_sb[:, slot, :])
            v.tensor_copy(rjunk_sb[:, :], rsume_sb[:, :])
            v.scalar_tensor_tensor(out=prb_sb[:, 1, :], in0=prb_sb[:, 1, :],
                                   scalar=m8_sb[:, 3:4], in1=prb_sb[:, 1, :],
                                   op0=ALU.is_ge, op1=ALU.mult).then_inc(sRGt.h, 1)
            v.wait_ge(sRTr.h, 15)
            v.tensor_copy(self.gate_fm[:, 14 * 128:15 * 128], gtp[:, 0, 0:128]).then_inc(sRCp.h, 1)
            v.wait_ge(sRTr.h, 16)
            v.tensor_copy(self.gate_fm[:, 15 * 128:16 * 128], gtp[:, 1, 0:128]).then_inc(sRCp.h, 1)
        sRMu.n = 16
        sRRc.n = 16
        sRGt.n = 16
        sRCp.n = 16
        self.router_done = sRCp
        self._close(rctx)
        self.tap("gate", lambda: self.gate_fm.ap(), [16, TLOC], F32)

    def emit_moe(self):
        b = self.block
        pctx = self.moe_ctx
        WAVES = [4, 4, 4, 3, 3][:self.moe_waves]
        EXPERTS = ([("s", 0), ("s", 1)] + [("r", i) for i in range(16)])[:sum(WAVES)]
        NW = max(WAVES)
        wup = self.sb("wup", [128, NW, 8, 512], BF16, ctx=pctx)
        wdn = self.sb("wdn", [128, NW, 4, 1024], BF16, ctx=pctx)
        gstage = self.sb("gstage", [1, 2, 1024], F32, ctx=pctx)
        gelu_sb = self.sb("gelu", [128, 1024], F32, ctx=pctx)
        act_sb = self.sb("mact", [128, 2, 4, 256], BF16, ctx=pctx)
        pu = self.ps("mpu", [128, 4, 256], ctx=pctx)
        gbp = self.ps("gbp", [128, 256], ctx=pctx)
        pd = self.ps("mpd", [128, 8, 256], ctx=pctx)
        sMW = self.sem("sMW")
        sMU = self.sem("sMU")
        sMG = self.sem("sMG")
        sMA = self.sem("sMA")
        sMB = self.sem("sMB")
        sMD = self.sem("sMD")
        sME = self.sem("sME")
        sGS = [self.sem(f"sGS{i}") for i in range(2)]
        gate_sem = self.router_done

        sched_m_of = {}
        cumd = []
        tot = 0
        for w_, wn_ in enumerate(WAVES):
            for k_ in range(8):
                tot += wn_
                cumd.append(tot)

        def moe_down(t, m, entry):
            w, k, i, (typ, ei) = entry
            g = w * 8 + k
            t.wait_ge(sMA.h, m + 1)
            if i == 0 and g >= 1:
                t.wait_ge(sME.h, g)
            for es in range(4):
                for dco in range(8):
                    ins = t.matmul(pd[:, dco, :], wdn[:, i, es, dco * 128:(dco + 1) * 128],
                                   act_sb[:, m % 2, es, :],
                                   start=(i == 0 and es == 0 and dco % 2 == 0),
                                   stop=False, skip_group_check=True)
            ins.then_inc(sMD.h, 1)

        sched = []
        e0 = 0
        for w, wn in enumerate(WAVES):
            for k in range(8):
                for i in range(wn):
                    sched.append((w, k, i, EXPERTS[e0 + i]))
            e0 += wn
        M = len(sched)
        wn_of = [WAVES[e[0]] for e in sched]

        cum_m_tt = cumd

        wave_r = []
        e0 = 0
        for wn in WAVES:
            rts = [EXPERTS[e0 + i][1] for i in range(wn) if EXPERTS[e0 + i][0] == "r"]
            wave_r.append((min(rts) if rts else 0, len(rts)))
            e0 += wn

        @b.gpsimd
        def _(g):
            g.wait_ge(gate_sem.h, 16)
            e0 = 0
            loads = []
            for w, wn in enumerate(WAVES):
                if w >= 1:
                    last_m = sum(WAVES[:w]) * 8
                    g.wait_ge(sMD.h, last_m)
                for i in range(wn):
                    typ, ei = EXPERTS[e0 + i]
                    upd = self.w_d["l1_su"][ei] if typ == "s" else self.w_d["l1_ru"][ei]
                    dnd = self.w_d["l1_sd"][ei] if typ == "s" else self.w_d["l1_rd"][ei]
                    sMW.dma(g.dma_start(out=wup[:, i, :, :],
                                        in_=upd.rearrange("(c p) e -> p c e", p=128)))
                    sMW.dma(g.dma_start(out=wdn[:, i, :, :],
                                        in_=dnd.rearrange("(s p) o -> p s o", p=128)))
                e0 += wn
                loads.append(sMW.n)
                r0, nr = wave_r[w]
                for k in range(8):
                    gg = w * 8 + k
                    slot = gg % 2
                    if gg >= 2:
                        g.wait_ge(sMB.h, cum_m_tt[gg - 2])
                    if nr:
                        g.dma_start(out=gstage[0:1, slot, 0:nr * 256],
                                    in_=self.gate_fm[r0:r0 + nr, k * 256:(k + 1) * 256]).then_inc(sGS[slot].h, 16)
                    else:
                        g.dma_start(out=gstage[0:1, slot, 0:256],
                                    in_=self.gate_fm[0:1, k * 256:(k + 1) * 256]).then_inc(sGS[slot].h, 16)
                    sGS[slot].n += 16
            self.moe_loads = loads

        @b.tensor
        def _(t):
            t.wait_ge(gate_sem.h, 16)
            t.wait_ge(self.sG.h, self.init_gate)
            for m, (w, k, i, (typ, ei)) in enumerate(sched):
                if i == 0 and k == 0 and m >= 1:
                    moe_down(t, m - 1, sched[m - 1])
                if i == 0:
                    t.wait_ge(sMW.h, self.moe_loads[w])
                ts = slice(k * 256, (k + 1) * 256)
                if m >= 1:
                    t.wait_ge(sMG.h, m)
                for es in range(4):
                    for dc in range(8):
                        ins = t.matmul(pu[:, es, :], wup[:, i, dc, es * 128:(es + 1) * 128],
                                       self.xn_sb[:, dc, ts], start=(dc == 0), stop=(dc == 7))
                ins.then_inc(sMU.h, 1)
                if typ == "r":
                    gg = w * 8 + k
                    if m >= 1:
                        t.wait_ge(sMA.h, m)
                    t.wait_ge(sGS[gg % 2].h, 16 * (gg // 2 + 1))
                    r0 = wave_r[w][0]
                    t.matmul(gbp[:, :], self.onesrow_f[:, :],
                             gstage[0:1, gg % 2, (ei - r0) * 256:(ei - r0 + 1) * 256],
                             start=True, stop=True).then_inc(sMB.h, 1)
                else:
                    t.nop().then_inc(sMB.h, 1)
                if m >= 1 and not (i == 0 and k == 0):
                    moe_down(t, m - 1, sched[m - 1])
            moe_down(t, M - 1, sched[M - 1])
        sMU.n = M
        sMB.n = M
        sMD.n = M

        @b.scalar
        def _(s):
            for m, (w, k, i, (typ, ei)) in enumerate(sched):
                s.wait_ge(sMU.h, m + 1)
                if m >= 1:
                    s.wait_ge(sMA.h, m)
                if typ == "s":
                    if m >= 2:
                        s.wait_ge(sMD.h, m - 1)
                    s.activation(out=act_sb[:, m % 2, :, :], in_=pu.ap()[:, :, :],
                                 func=AF.Gelu).then_inc(sMG.h, 1)
                else:
                    s.activation(out=gelu_sb.ap().rearrange("p (a b) -> p a b", a=4),
                                 in_=pu.ap()[:, :, :], func=AF.Gelu).then_inc(sMG.h, 1)
        sMG.n = M

        @b.vector
        def _(v):
            for m, (w, k, i, (typ, ei)) in enumerate(sched):
                v.wait_ge(sMG.h, m + 1)
                if typ == "r":
                    v.wait_ge(sMB.h, m + 1)
                    if m >= 2:
                        v.wait_ge(sMD.h, m - 1)
                    v.tensor_mul(act_sb[:, m % 2, :, :],
                                 gelu_sb.ap().rearrange("p (a b) -> p a b", a=4),
                                 gbp.ap()[:, :].rearrange("p (a b) -> p a b", a=1).broadcast_to([128, 4, 256])).then_inc(sMA.h, 1)
                else:
                    v.nop().then_inc(sMA.h, 1)
                if i == wn_of[m] - 1:
                    g = w * 8 + k
                    v.wait_ge(sMD.h, cumd[g])
                    ts = slice(k * 256, (k + 1) * 256)
                    for dco in range(8):
                        ins = v.scalar_tensor_tensor(out=self.x_sb[:, dco, ts], in0=pd[:, dco, :],
                                                     scalar=1.0, in1=self.x_sb[:, dco, ts],
                                                     op0=ALU.mult, op1=ALU.add)
                    ins.then_inc(sME.h, 1)
        sMA.n = M
        sME.n = 8 * len(WAVES)
        self.x_final_gate = (sME, 8 * len(WAVES))
        self._moe_cumd = cumd
        self._close(pctx)
        self.tap("xm", lambda: self.x_sb.ap(), [128, 8, TLOC], F32)



    def emit_out(self):
        b = self.block
        gsem, gval = self.x_final_gate

        @b.sync
        def _(s):
            s.wait_ge(gsem.h, gval)
            self.sY.dma(s.dma_start(out=self.y_d.rearrange("(c p) t -> p c t", p=128),
                                    in_=self.x_sb[:]))
            s.wait_ge(self.sY.h, self.sY.n)

        @b.gpsimd
        def _(g):
            g.wait_ge(self.sY.h, self.sY.n)

    def emit_taps(self):
        if not self.tap_outs:
            return
        b = self.block

        @b.gpsimd
        def _(g):
            for s in self.all_sems:
                if s.n > 0:
                    g.wait_ge(s.h, s.n)
            for name, (dram, ap_fn) in self.tap_outs.items():
                self.sY.dma(g.dma_start(out=dram, in_=ap_fn()))
            g.wait_ge(self.sY.h, self.sY.n)


_CACHE = {}


def get_nc(taps=(), stop_after="out", moe_waves=5, router_stage=4):
    key = (tuple(sorted(taps)), stop_after, moe_waves, router_stage)
    if key not in _CACHE:
        _CACHE[key] = MK(taps=taps, stop_after=stop_after, moe_waves=moe_waves,
                         router_stage=router_stage).build()
    return _CACHE[key]


def run(inputs, taps=(), stop_after="out", raw=False, moe_waves=5, router_stage=4, **kw):
    nonce = int(np.random.randint(1, 2 ** 30))
    in_maps = host_prepare(inputs, nonce)
    nc = get_nc(taps, stop_after, moe_waves, router_stage)
    res = run_bass_kernel_spmd(nc, in_maps, core_ids=list(range(8)), **kw)
    if raw:
        return res
    return host_assemble(res.results)


def kernel(**inputs):
    nonce = int(np.random.randint(1, 2 ** 30))
    in_maps = host_prepare(inputs, nonce)
    nc = get_nc()
    res = run_bass_kernel_spmd(nc, in_maps, core_ids=list(range(8)))
    return host_assemble(res.results)
